# version 04
# baseline (speedup 1.0000x reference)
import numpy as np
import ml_dtypes

import concourse.bacc as bacc
import concourse.mybir as mybir
from concourse.bass_utils import run_bass_kernel_spmd

F32 = mybir.dt.float32
BF16 = mybir.dt.bfloat16
I16 = mybir.dt.int16
OP = mybir.AluOpType
AX = mybir.AxisListType
ACT = mybir.ActivationFunctionType
BF = ml_dtypes.bfloat16

NCORE = 8
D = 1024
DC = 8
NT = 2048
NB = 4096
N = 16384
E = 16
F = 1408
FC = 11
CAP = 2560
ROWS = 21
SLOTPAD = ROWS * 128
NCH = 256
EPS = 1e-5
KB = 1024

DBG = False
STOP_AFTER = "full"

ENGS = ("sync", "scalar", "vector", "tensor", "gpsimd")


class HazardGuard:

    RW = {
        "tensor_tensor": ([0], [1, 2]),
        "tensor_copy": ([0], [1]),
        "tensor_scalar": ([0], [1, 2, 3]),
        "reduce_sum": ([0], [1]),
        "reduce_max": ([0], [1]),
        "scalar_tensor_tensor": ([0], [1, 2, 3]),
        "tensor_add": ([0], [1, 2]),
        "reciprocal": ([0], [1]),
        "memset": ([0], []),
        "tensor_tensor_scan": ([0], [1, 2, 3]),
        "activation": ([0], [1, 3, 4]),
    }

    def __init__(self, eng, is_scalar, junk):
        self._eng = eng
        self._sc = is_scalar
        self._junk = junk
        self._w1 = frozenset()
        self._w2 = frozenset()
        self._slot = 0
        self.n_spacers = 0

    @staticmethod
    def _names(objs):
        out = set()
        for a in objs:
            t = getattr(a, "tensor", None)
            if t is not None:
                out.add(t.name)
            elif hasattr(a, "name") and hasattr(a, "shape"):
                out.add(a.name)
        return out

    def _spacer(self):
        j = self._junk
        for _ in range(2):
            if self._sc:
                self._eng.activation(j[:, 1, :], j[:, 2, :], ACT.Copy)
            else:
                s = self._slot
                self._slot ^= 1
                self._eng.tensor_scalar(j[:, s, :], j[:, 3, :], 1.0, None, OP.mult)
            self.n_spacers += 1
        self._w2 = self._w1 = frozenset(("__junk__",))

    def __getattr__(self, nm):
        fn = getattr(self._eng, nm)
        if nm == "dma_start":
            def dwrap(*args, **kw):
                rd = self._names(list(args[1:2]) + [kw.get("in_")])
                if rd & (self._w1 | self._w2):
                    self._spacer()
                return fn(*args, **kw)
            return dwrap
        spec = self.RW.get(nm)
        if spec is None:
            return fn
        widx, ridx = spec
        def wrapped(*args, **kw):
            reads = [args[i] for i in ridx if i < len(args)]
            for k in ("bias", "scale", "in1", "d0", "d1"):
                if kw.get(k) is not None:
                    reads.append(kw[k])
            writes = [args[i] for i in widx if i < len(args)]
            if kw.get("out") is not None:
                writes.append(kw["out"])
            if kw.get("accum_out") is not None:
                writes.append(kw["accum_out"])
            rn = self._names(reads)
            if rn & (self._w1 | self._w2):
                self._spacer()
            ins = fn(*args, **kw)
            self._w2 = self._w1
            self._w1 = frozenset(self._names(writes))
            return ins
        return wrapped


def _sp_tile(vals):
    n = len(vals)
    t = np.zeros((16, n // 16), np.int16)
    t[np.arange(n) % 16, np.arange(n) // 16] = np.asarray(vals, np.int64)
    return np.tile(t, (8, 1))


def build_consts(core):
    p = np.arange(128)
    c = {}
    c["ident"] = np.eye(128, dtype=BF)
    c["identf"] = np.eye(128, dtype=np.float32)
    order = 2 * (p % 64) + p // 64
    c["lts"] = (order[:, None] < order[None, :]).astype(BF)
    c["ones_col"] = np.ones((128, 1), dtype=BF)
    c["ones_row"] = np.ones((1, 128), np.float32)
    s0 = np.zeros((128, 64), np.float32)
    s1 = np.zeros((128, 64), np.float32)
    s0[np.arange(64), np.arange(64)] = 1.0
    s1[64 + np.arange(64), np.arange(64)] = 1.0
    c["sel_k0"], c["sel_k1"] = s0, s1
    me = np.zeros((NCH, 16), np.float32)
    mo = np.zeros((NCH, 16), np.float32)
    for u in range(32):
        (me if u % 2 == 0 else mo)[32 * core + u, u // 2] = 1.0
    c["m_even"], c["m_odd"] = me, mo
    c["iota_row"] = np.tile(np.arange(512, dtype=np.float32)[None, :], (128, 1))
    c["tokbase"] = (np.arange(NCH)[None, :] * 64 + (p % 64)[:, None]).astype(np.float32)
    c["even_col"] = (p[:, None] < 64).astype(np.float32)
    c["odd_col"] = (p[:, None] >= 64).astype(np.float32)
    c["own_idx"] = _sp_tile(np.arange(NT))
    b, h = core // 2, core % 2
    own_g = np.arange(b * NB + h * NT, b * NB + (h + 1) * NT)
    par_g = np.arange(b * NB + (1 - h) * NT, b * NB + (2 - h) * NT)
    c["batch_idx"] = _sp_tile(np.concatenate([own_g, par_g]) + 1)
    c["parrow"] = np.tile((np.arange(16) % 2).astype(np.float32)[None, :], (128, 1))
    c["halfrow"] = np.tile(((np.arange(16) // 2) * CAP).astype(np.float32)[None, :], (128, 1))
    c["thresh"] = np.tile((128.0 * (1 + np.arange(ROWS))).astype(np.float32)[None, :], (128, 1))
    e0 = np.zeros((128, 16), np.float32); e0[:, 2 * core] = 1.0
    e1 = np.zeros((128, 16), np.float32); e1[:, 2 * core + 1] = 1.0
    c["ec0"], c["ec1"] = e0, e1
    return c


CONST_SHAPES = {
    "ident": ([128, 128], BF16), "identf": ([128, 128], F32),
    "lts": ([128, 128], BF16),
    "ones_col": ([128, 1], BF16), "ones_row": ([1, 128], F32),
    "sel_k0": ([128, 64], F32), "sel_k1": ([128, 64], F32),
    "iota_row": ([128, 512], F32),
    "tokbase": ([128, NCH], F32),
    "even_col": ([128, 1], F32), "odd_col": ([128, 1], F32),
    "own_idx": ([128, 128], I16), "batch_idx": ([128, 256], I16),
    "ec0": ([128, 16], F32), "ec1": ([128, 16], F32),
    "parrow": ([128, 16], F32), "halfrow": ([128, 16], F32),
    "thresh": ([128, ROWS], F32),
}


class Emit:

    def __init__(self, name, eng, sems):
        self.name = name
        self.eng = eng
        self.sems = sems
        self.cnt = {k: 0 for k in sems}
        self.barno = 0

    def wait(self, owner, sem, val=None):
        v = self.cnt[sem] if val is None else val
        if self.name == owner and v > 0:
            self.eng.wait_ge(self.sems[sem], v)

    def dma(self, owner, sem, out, in_, **kw):
        self.cnt[sem] += 16
        if self.name == owner:
            self.eng.dma_start(out=out, in_=in_, **kw).then_inc(self.sems[sem], 16)
        return self.cnt[sem]

    def op(self, owner, sem, fn, n=1):
        if sem is not None:
            self.cnt[sem] += n
        if self.name == owner:
            ins = fn(self.eng)
            if sem is not None and ins is not None:
                ins.then_inc(self.sems[sem], n)
        return self.cnt[sem] if sem is not None else None

    def run(self, owner, fn):
        if self.name == owner:
            fn(self.eng)

    def barrier(self):
        self.barno += 1
        self.eng.sem_inc(self.sems["b_" + self.name], 1)
        for other in ENGS:
            if other != self.name:
                self.eng.wait_ge(self.sems["b_" + other], self.barno)


def build_program():
    nc = bacc.Bacc("TRN2", target_bir_lowering=False, debug=False,
                   detect_race_conditions=False)

    ap = {}
    ap["x"] = nc.dram_tensor("x_own", [NT, D], F32, kind="ExternalInput").ap()
    ap["wqkv"] = nc.dram_tensor("w_qkv", [D, 3 * D], F32, kind="ExternalInput").ap()
    ap["outw"] = nc.dram_tensor("out_w", [D, D], F32, kind="ExternalInput").ap()
    ap["outb"] = nc.dram_tensor("out_b_bc", [128, D], F32, kind="ExternalInput").ap()
    ap["rw"] = nc.dram_tensor("router_w", [D, E], F32, kind="ExternalInput").ap()
    ap["rb"] = nc.dram_tensor("router_b_bc", [128, E], F32, kind="ExternalInput").ap()
    ap["wg"] = nc.dram_tensor("wg_pair", [2, D, F], F32, kind="ExternalInput").ap()
    ap["wu"] = nc.dram_tensor("wu_pair", [2, D, F], F32, kind="ExternalInput").ap()
    ap["wd"] = nc.dram_tensor("wd_pair", [2, F, D], F32, kind="ExternalInput").ap()
    ap["swg"] = nc.dram_tensor("sw_gate", [D, F], F32, kind="ExternalInput").ap()
    ap["swu"] = nc.dram_tensor("sw_up", [D, F], F32, kind="ExternalInput").ap()
    ap["swd"] = nc.dram_tensor("sw_down", [F, D], F32, kind="ExternalInput").ap()
    for k, (shp, dt) in CONST_SHAPES.items():
        ap[k] = nc.dram_tensor("c_" + k, shp, dt, kind="ExternalInput").ap()
    ap["m_even"] = nc.dram_tensor("c_m_even", [NCH, 16], F32, kind="ExternalInput").ap()
    ap["m_odd"] = nc.dram_tensor("c_m_odd", [NCH, 16], F32, kind="ExternalInput").ap()
    ap["out"] = nc.dram_tensor("out", [NT, D], F32, kind="ExternalOutput").ap()

    dr = {}
    dr["ag_in_xn"] = nc.dram_tensor("ag_in_xn", [NT, D], BF16)
    dr["ag_out_xn"] = nc.dram_tensor("ag_out_xn", [N + 1, D], BF16)
    dr["ag_in_lg"] = nc.dram_tensor("ag_in_lg", [64, 32 * E], F32)
    dr["ag_out_lg"] = nc.dram_tensor("ag_out_lg", [512, 32 * E], F32)
    dr["ag_in_eoA"] = nc.dram_tensor("ag_in_eoA", [CAP, D], BF16)
    dr["ag_out_eoA"] = nc.dram_tensor("ag_out_eoA", [NCORE * CAP, D], BF16)
    dr["ag_in_eoB"] = nc.dram_tensor("ag_in_eoB", [CAP, D], BF16)
    dr["ag_out_eoB"] = nc.dram_tensor("ag_out_eoB", [NCORE * CAP, D], BF16)
    dr["q_scr"] = nc.dram_tensor("q_scr", [DC, 128, NT], BF16)
    dr["attn_scr"] = nc.dram_tensor("attn_scr", [NT, D], BF16)
    dr["smlp_scr"] = nc.dram_tensor("smlp_scr", [NT, D], BF16)

    dbg = {}
    if DBG:
        def dout(name, shp, dt=F32):
            dbg[name] = nc.dram_tensor("dbg_" + name, shp, dt, kind="ExternalOutput").ap()
        dout("stats", [128, 6, 16])
        dout("lg", [512, 32 * E])
        dout("pos", [128, 2, 16])
        dout("slots", [128, 2, 128])
        dout("wm", [128, 12, 16])
        dout("eidx", [128, 2, 160], I16)
        dout("q", [DC, 128, NT], BF16)
        dout("attn", [NT, D], BF16)
        dout("smlp", [NT, D], BF16)
        dout("eoA", [CAP, D], BF16)
        dout("eoB", [CAP, D], BF16)
        dout("K", [128, 2 * DC * NT], BF16)
        dout("V", [128, 32 * D], BF16)
        dout("einp", [128, DC * 640], BF16)

    sb = {}
    arena_bytes = (nc.sbuf_bytes_remaining - 2048) // 32 * 32
    nc.alloc_sbuf_tensor("arena", [128, arena_bytes], mybir.dt.int8)
    arena_base = None
    for alloc in nc.m.functions[0].allocations:
        if isinstance(alloc, mybir.MemoryLocationSet) and alloc.memorylocations:
            ml = alloc.memorylocations[0]
            if ml.name.startswith("arena"):
                arena_base = ml.addr
    assert arena_base is not None

    def at(name, off, shape, dtype):
        t = nc.alloc_sbuf_tensor_at(name, shape, dtype, offset=arena_base + off)
        sb[name] = t
        return t

    o = [0]

    def pat(name, shape, dtype, nbytes):
        t = at(name, o[0], shape, dtype)
        o[0] += (nbytes + 31) // 32 * 32
        return t

    pat("ident", [128, 128], BF16, 256)
    pat("identf", [128, 128], F32, 512)
    pat("lts", [128, 128], BF16, 256)
    pat("ones_col", [128, 1], BF16, 32)
    pat("ones_row", [1, 128], F32, 512)
    pat("sel_k0", [128, 64], F32, 256)
    pat("sel_k1", [128, 64], F32, 256)
    pat("m_even", [128, 2, 16], F32, 128)
    pat("m_odd", [128, 2, 16], F32, 128)
    pat("iota_row", [128, 512], F32, 2048)
    pat("tokbase", [128, NCH], F32, 1024)
    pat("even_col", [128, 1], F32, 32)
    pat("odd_col", [128, 1], F32, 32)
    pat("own_idx", [128, 128], I16, 256)
    pat("batch_idx", [128, 256], I16, 512)
    pat("ec0", [128, 16], F32, 64)
    pat("ec1", [128, 16], F32, 64)
    pat("parrow", [128, 16], F32, 64)
    pat("halfrow", [128, 16], F32, 64)
    pat("thresh", [128, ROWS], F32, 128)
    pat("rb", [128, E], F32, 64)
    pat("rwb", [128, DC, E], BF16, 256)
    for nm in ["mu", "isg", "st1", "st2", "st3", "nms", "isg1",
               "m1t", "m2t", "e1t", "e2t", "w1t", "w2t", "zt", "dent",
               "wA0", "wB0", "wA1", "wB1", "iA0", "iB0", "iA1", "iB1",
               "pos0", "pos1", "attn_m", "attn_l", "linv",
               "scr16", "scr16b", "win0", "win1",
               "par1", "par2", "rb1", "rb2"]:
        pat(nm, [128, 16], F32, 64)
    pat("lg_own", [128, 16, E], F32, 1024)
    pat("eidx0", [128, 160], I16, 320)
    pat("eidx1", [128, 160], I16, 320)
    for nm in ["cA0", "cB0", "cA1", "cB1"]:
        pat(nm, [128, 128], I16, 256)
    pat("qe", [128, 512], BF16, 1024)
    pat("qe2", [128, 512], BF16, 1024)
    pat("junk", [128, 4, 4], F32, 64)

    WX = (o[0] + 127) // 128 * 128
    RK = WX + 16 * KB
    RV = RK + 64 * KB
    RC = RV + 64 * KB
    assert RC + 48 * KB <= arena_bytes, (RC, arena_bytes)

    at("xt", WX, [128, 2, D], F32)
    at("sqscr", WX + 8 * KB, [128, 2, D], BF16)
    at("xnt", WX + 12 * KB, [128, 512], BF16)
    at("zero_row", WX + 13 * KB, [1, D], BF16)
    at("xnf", WX, [128, 2, DC, 512], BF16)
    at("xnf_o", WX, [128, DC, 256], BF16)
    at("ohk", WX, [128, NCH, E], BF16)
    at("member0", WX + 8 * KB, [128, NCH], F32)
    at("member1", WX + 9 * KB, [128, NCH], F32)
    at("runs", WX + 10 * KB, [128, 512], F32)
    at("runsT", WX + 12 * KB, [128, 4, 128], F32)
    at("woT", WX, [128, DC, D], BF16)

    at("K", RK, [128, 2, DC, NT], BF16)
    at("V", RV, [128, 2, 16, D], BF16)
    at("swg", RK + 32 * KB, [128, DC, F], BF16)
    at("swu", RV + 32 * KB, [128, DC, F], BF16)
    at("einp", RK, [128, 4, DC, CAP // 4], BF16)
    at("ewd", RV, [128, FC, D], BF16)
    at("eh", RV + 22 * KB, [128, FC, 512], BF16)
    at("eo_t", RV + 33 * KB, [128, 2, D], BF16)
    at("gA0", RK, [128, 16, D], BF16)
    at("gB0", RK + 32 * KB, [128, 16, D], BF16)
    at("gA1", RV, [128, 16, D], BF16)
    at("gB1", RV + 32 * KB, [128, 16, D], BF16)

    at("wqkv", RC, [128, DC, 3 * D], BF16)
    at("swd", RC, [128, FC, D], BF16)
    at("h", RC + 22 * KB, [128, FC, 512], BF16)
    at("outw_n", RC, [128, DC, D], BF16)
    at("lg_pm", RC, [128, NCH, E], F32)
    at("oh1", RC + 16 * KB, [128, NCH, E], BF16)
    at("oh2", RC + 24 * KB, [128, NCH, E], BF16)
    at("excl", RC, [128, NCH, E], BF16)
    at("scrROW", RC, [1, NCH * ROWS], F32)
    at("scrH", RC + 8 * KB, [128, 128, E], F32)
    at("trow", RC + 16 * KB, [1, 4096], F32)
    at("brow", RC + 32 * KB, [1, 4096], F32)
    at("bbc", RC + 16 * KB, [128, NCH], F32)
    at("slotA", RC + 17 * KB, [128, 128], F32)
    at("slotB", RC + 17 * KB + 512, [128, 128], F32)
    at("idxpre", RC + 18 * KB, [128, 168], F32)
    at("bbc2", RC + 19 * KB, [128, NCH], F32)
    at("rA_row", RC + 20 * KB, [1, 256], F32)
    at("stg", RC + 21 * KB, [128, 128], F32)
    at("stg2", RC + 21 * KB + 512, [128, 128], F32)
    at("posTj", RC + 22 * KB, [128, 2, 64], F32)
    at("rhs2", RC + 43 * KB, [128, NCH, 2], F32)
    at("ohsl", RC + 25 * KB, [128, 4, 4, 128], F32)
    at("rv_row", RC + 33 * KB, [1, 512], F32)
    at("pose0", RC + 35 * KB, [128, NCH], F32)
    at("pose1", RC + 36 * KB, [128, NCH], F32)
    at("posm", RC + 37 * KB, [128, NCH], F32)
    at("rowi", RC + 38 * KB, [128, NCH], F32)
    at("val", RC + 39 * KB, [128, NCH], F32)
    at("rvv", RC + 40 * KB, [128, 4], F32)
    at("Rq", RC + 40 * KB + 32, [128, 32], F32)
    at("sAT", RC + 40 * KB + 160, [128, 32], F32)
    at("pos_pair", RC + 25 * KB, [128, NCH], F32)
    at("m1pm", RC + 32 * KB, [128, NCH], F32)
    at("m2pm", RC + 33 * KB, [128, NCH], F32)
    at("be_row0", RC + 23 * KB, [1, 256], F32)
    at("be_row1", RC + 24 * KB, [1, 256], F32)
    at("S", RC, [128, NB], F32)
    at("P", RC + 16 * KB, [128, NB], BF16)
    at("PT", RC + 24 * KB, [128, 32, 256], BF16)
    at("Qt", RC + 40 * KB, [128, DC, 128], BF16)
    at("Osb", RC + 42 * KB, [128, DC, 256], BF16)
    at("ewg", RC, [128, DC, F], BF16)
    at("ewu", RC + 22 * KB, [128, DC, F], BF16)
    at("acc", RC, [128, D], F32)
    at("cx", RC + 4 * KB, [128, 2, D], F32)
    at("ca", RC + 12 * KB, [128, 2, D], BF16)
    at("cs", RC + 16 * KB, [128, 2, D], BF16)
    at("outb", RC + 20 * KB, [128, D], F32)

    ps = {}
    ps["a"] = nc.alloc_psum_tensor("ps_a", [128, 4, 512], F32)
    ps["b"] = nc.alloc_psum_tensor("ps_b", [128, 1, 512], F32)
    ps["c"] = nc.alloc_psum_tensor("ps_c", [128, 512], F32)
    ps["t"] = nc.alloc_psum_tensor("ps_t", [128, 2, 1024], BF16)

    sem_names = ["s_xa", "s_xb", "s_lg", "s_qt", "s_ob",
                 "s_wq", "s_ws", "s_we", "s_wo", "s_cst", "s_cc",
                 "s_gf", "s_qa", "s_qb", "s_qx", "s_pay", "s_zr", "s_out",
                 "s_dbg", "s_rep",
                 "t_mm", "v_ev", "a_ev",
                 "b_sync", "b_scalar", "b_vector", "b_tensor", "b_gpsimd"]

    import contextlib
    stack = contextlib.ExitStack()
    sems = {s: stack.enter_context(nc.semaphore(s)) for s in sem_names}
    block = stack.enter_context(nc.Block())

    def emit_for(name):
        def f(eng):
            _program(Emit(name, eng, sems), ap, dr, sb, ps, dbg)
        return f

    block.sync(emit_for("sync"))
    block.scalar(emit_for("scalar"))
    block.vector(emit_for("vector"))
    block.tensor(emit_for("tensor"))
    block.gpsimd(emit_for("gpsimd"))

    stack.close()
    nc.compile()
    return nc



def _program(Em, ap, dr, sb, ps, dbg):
    T = sb
    if Em.name == "vector":
        Em.eng = HazardGuard(Em.eng, False, T["junk"])
        Em.eng._eng.memset(T["junk"][:, 3, :], 1.0)
    elif Em.name == "scalar":
        Em.eng = HazardGuard(Em.eng, True, T["junk"])
        Em.eng._eng.memzero(T["junk"][:, 2, :])
    bank_free = {}

    def mm_acc(bank_key, bank_ap, pairs, waits=()):
        if bank_key in bank_free:
            sem, cv = bank_free[bank_key]
            Em.wait("tensor", sem, cv)
        for w in waits:
            Em.wait("tensor", w[0], w[1])
        nmm = len(pairs)
        for i, (l, r) in enumerate(pairs):
            last = i == nmm - 1
            Em.op("tensor", "t_mm" if last else None,
                  lambda e, l=l, r=r, i=i, last=last: e.matmul(
                      bank_ap, l, r, start=(i == 0), stop=last,
                      skip_group_check=True))
        return Em.cnt["t_mm"]

    def evac(owner, bank_key, tcnt, fn):
        sem = "v_ev" if owner == "vector" else "a_ev"
        if tcnt is not None:
            Em.wait(owner, "t_mm", tcnt)
        c = Em.op(owner, sem, fn)
        if bank_key is not None:
            bank_free[bank_key] = (sem, c)
        return c

    def transpose(bank_key, out_ap, in_ap, ident_ap, waits=()):
        if bank_key in bank_free:
            sem, cv = bank_free[bank_key]
            Em.wait("tensor", sem, cv)
        for w in waits:
            Em.wait("tensor", w[0], w[1])
        return Em.op("tensor", "t_mm",
                     lambda e: e.transpose(out_ap, in_ap, ident_ap))

    for k in CONST_SHAPES:
        Em.dma("sync", "s_cst", T[k][:], ap[k])
    Em.dma("sync", "s_cst", T["m_even"][:], ap["m_even"].rearrange("(a p) e -> p a e", p=128))
    Em.dma("sync", "s_cst", T["m_odd"][:], ap["m_odd"].rearrange("(a p) e -> p a e", p=128))
    Em.dma("sync", "s_cst", T["rb"][:], ap["rb"])
    Em.wait("tensor", "s_cst")
    Em.wait("vector", "s_cst")

    Em.dma("gpsimd", "s_wq", T["wqkv"][:], ap["wqkv"].rearrange("(c p) n -> p c n", p=128))
    Em.wait("gpsimd", "s_wq")
    Em.dma("gpsimd", "s_ws", T["swg"][:], ap["swg"].rearrange("(c p) n -> p c n", p=128))
    Em.wait("gpsimd", "s_ws")
    Em.dma("gpsimd", "s_ws", T["swu"][:], ap["swu"].rearrange("(c p) n -> p c n", p=128))
    Em.wait("gpsimd", "s_ws")
    Em.dma("gpsimd", "s_wo", T["rwb"][:], ap["rw"].rearrange("(c p) e -> p c e", p=128))

    Em.op("vector", "v_ev", lambda e: e.memset(T["zero_row"][:], 0.0))
    Em.wait("gpsimd", "v_ev")
    Em.dma("gpsimd", "s_zr", dr["ag_out_xn"].ap()[0:1, :], T["zero_row"][:])

    Em.run("scalar", lambda e: e.preload_act_table(ACT.Sqrt))

    xv = ap["x"].rearrange("(i p) d -> i p d", p=128)
    xnv = dr["ag_in_xn"].ap().rearrange("(i p) d -> i p d", p=128)
    for i in range(16):
        sx = "s_xa" if i % 2 == 0 else "s_xb"
        Em.dma("sync", sx, T["xt"][:, i % 2, :], xv[i])
        Em.wait("scalar", sx)
        Em.op("scalar", "a_ev", lambda e, i=i: e.activation(
            T["sqscr"][:, i % 2, :], T["xt"][:, i % 2, :], ACT.Square,
            accum_out=T["st2"][:, i:i + 1]))
        Em.wait("vector", sx)
        Em.op("vector", "v_ev", lambda e, i=i: e.reduce_sum(
            T["st1"][:, i:i + 1], T["xt"][:, i % 2, :], axis=AX.X))
        Em.wait("sync", "a_ev", Em.cnt["a_ev"] - 1)
        Em.wait("sync", "v_ev", Em.cnt["v_ev"] - 1)
    Em.wait("vector", "a_ev")

    def _stats(e):
        e.tensor_scalar(T["mu"][:], T["st1"][:], 1.0 / D, None, OP.mult)
        e.tensor_scalar(T["st3"][:], T["st2"][:], 1.0 / D, None, OP.mult)
        e.tensor_tensor(T["st1"][:], T["mu"][:], T["mu"][:], OP.mult)
        e.tensor_tensor(T["st3"][:], T["st3"][:], T["st1"][:], OP.subtract)
        e.tensor_scalar(T["st3"][:], T["st3"][:], EPS, None, OP.add)
        return e.reciprocal(T["st2"][:], T["st3"][:])
    Em.op("vector", "v_ev", _stats)
    Em.wait("scalar", "v_ev")
    Em.op("scalar", "a_ev", lambda e: e.activation(T["isg"][:], T["st2"][:], ACT.Sqrt))
    Em.wait("vector", "a_ev")

    def _newton(e):
        e.tensor_tensor(T["st1"][:], T["isg"][:], T["isg"][:], OP.mult)
        e.tensor_tensor(T["st1"][:], T["st1"][:], T["st3"][:], OP.mult)
        e.tensor_scalar(T["st1"][:], T["st1"][:], -0.5, 1.5, OP.mult, OP.add)
        e.tensor_tensor(T["isg"][:], T["isg"][:], T["st1"][:], OP.mult)
        e.tensor_tensor(T["nms"][:], T["mu"][:], T["isg"][:], OP.mult)
        e.tensor_scalar(T["nms"][:], T["nms"][:], -1.0, None, OP.mult)
        return e.tensor_scalar(T["isg1"][:], T["isg"][:], 1.0, 1.0, OP.mult, OP.add)
    Em.op("vector", "v_ev", _newton)

    for i in range(16):
        sx = "s_xa" if i % 2 == 0 else "s_xb"
        Em.dma("sync", sx, T["xt"][:, i % 2, :], xv[i])
        for hh in range(2):
            Em.wait("vector", sx)
            Em.wait("vector", "s_qx")
            Em.op("vector", "v_ev", lambda e, i=i, hh=hh: e.tensor_scalar(
                T["xnt"][:, :], T["xt"][:, i % 2, hh * 512:(hh + 1) * 512],
                T["isg"][:, i:i + 1], T["nms"][:, i:i + 1], OP.mult, OP.add))
            Em.wait("scalar", "v_ev")
            Em.dma("scalar", "s_qx", xnv[i][:, hh * 512:(hh + 1) * 512], T["xnt"][:, :])
        Em.wait("sync", "v_ev", Em.cnt["v_ev"] - 2)
    Em.wait("gpsimd", "s_qx")
    Em.op("gpsimd", "s_cc", lambda e: e.collective_compute(
        "AllGather", OP.bypass, replica_groups=[list(range(NCORE))],
        ins=[dr["ag_in_xn"].ap().opt()],
        outs=[dr["ag_out_xn"].ap()[1:N + 1, :].opt()]))
    cc_xn = Em.cnt["s_cc"]

    Em.barrier()
    if STOP_AFTER == "ph1":
        _dbg_stop(Em, ap, dr, T, dbg)
        return

    qsv = dr["q_scr"].ap()
    for g in range(4):
        Em.op("gpsimd", "s_gf", lambda e, g=g: e.dma_gather(
            out_ap=T["xnf"][:, g % 2, :, :], in_ap=dr["ag_in_xn"].ap(),
            idxs_ap=T["own_idx"][:, g * 32:(g + 1) * 32], num_idxs=512,
            num_idxs_reg=512, elem_size=D, transpose=True), 16)
        gf = Em.cnt["s_gf"]
        xg = T["xnf"][:, g % 2]
        Em.wait("tensor", "s_gf", gf)
        Em.wait("tensor", "s_wq", 16)
        Em.wait("tensor", "s_wo", 16)
        for tt in range(4):
            tc = mm_acc(("c",), ps["c"][:, 0:E],
                        [(xg[:, dc, tt * 128:(tt + 1) * 128], T["rwb"][:, dc, :])
                         for dc in range(DC)])
            evac("vector", ("c",), tc, lambda e, g=g, tt=tt: e.tensor_add(
                T["lg_own"][:, g * 4 + tt, :], ps["c"][:, 0:E], T["rb"][:]))
        for qc in range(DC):
            bk = ("a", qc % 4)
            tc = mm_acc(bk, ps["a"][:, qc % 4, :],
                        [(T["wqkv"][:, dc, qc * 128:(qc + 1) * 128], xg[:, dc, :])
                         for dc in range(DC)])
            qdst = T["qe"] if qc % 2 == 0 else T["qe2"]
            sq = "s_qa" if qc % 2 == 0 else "s_qb"
            Em.wait("scalar", sq)
            evac("scalar", bk, tc, lambda e, qdst=qdst: e.activation(
                qdst[:], ps["a"][:, qc % 4, :], ACT.Copy, scale=1.0 / 32.0))
            Em.dma("scalar", sq, qsv[qc, :, g * 512:(g + 1) * 512], qdst[:])
        for kc in range(DC):
            bk = ("a", kc % 4)
            tc = mm_acc(bk, ps["a"][:, kc % 4, :],
                        [(T["wqkv"][:, dc, D + kc * 128:D + (kc + 1) * 128], xg[:, dc, :])
                         for dc in range(DC)])
            evac("vector", bk, tc, lambda e, g=g, kc=kc: e.tensor_copy(
                T["K"][:, 0, kc, g * 512:(g + 1) * 512], ps["a"][:, kc % 4, :]))
        for tt in range(4):
            for vc in range(2):
                bk = ("b", 0)
                tc = mm_acc(bk, ps["b"][:, 0, :],
                            [(xg[:, dc, tt * 128:(tt + 1) * 128],
                              T["wqkv"][:, dc, 2 * D + vc * 512:2 * D + (vc + 1) * 512])
                             for dc in range(DC)])
                evac("vector", bk, tc, lambda e, g=g, tt=tt, vc=vc: e.tensor_copy(
                    T["V"][:, 0, g * 4 + tt, vc * 512:(vc + 1) * 512], ps["b"][:, 0, :]))
        Em.wait("gpsimd", "t_mm", Em.cnt["t_mm"])

    pay = dr["ag_in_lg"].ap().rearrange("j (c e) -> j c e", e=E)
    Em.wait("sync", "v_ev")
    Em.dma("sync", "s_pay", pay[:, 0::2, :], T["lg_own"][0:64, :, :])
    Em.dma("sync", "s_pay", pay[:, 1::2, :], T["lg_own"][64:128, :, :])
    Em.wait("gpsimd", "s_pay")
    Em.op("gpsimd", "s_cc", lambda e: e.collective_compute(
        "AllGather", OP.bypass, replica_groups=[list(range(NCORE))],
        ins=[dr["ag_in_lg"].ap().opt()],
        outs=[dr["ag_out_lg"].ap().opt()]))
    cc_lg = Em.cnt["s_cc"]

    Em.barrier()
    if STOP_AFTER == "ph2":
        _dbg_stop(Em, ap, dr, T, dbg)
        return

    Em.run("scalar", lambda e: e.preload_act_table(ACT.Silu))
    Em.dma("gpsimd", "s_ws", T["swd"][:], ap["swd"].rearrange("(c p) n -> p c n", p=128))
    Em.wait("gpsimd", "s_ws")
    smv = dr["smlp_scr"].ap().rearrange("(i p) d -> i p d", p=128)
    for g in range(4):
        Em.op("gpsimd", "s_gf", lambda e, g=g: e.dma_gather(
            out_ap=T["xnf"][:, g % 2, :, :], in_ap=dr["ag_in_xn"].ap(),
            idxs_ap=T["own_idx"][:, g * 32:(g + 1) * 32], num_idxs=512,
            num_idxs_reg=512, elem_size=D, transpose=True), 16)
        gf = Em.cnt["s_gf"]
        xg = T["xnf"][:, g % 2]
        Em.wait("tensor", "s_gf", gf)
        Em.wait("tensor", "s_ws", 32)
        for fc in range(FC):
            bkg = ("a", fc % 2)
            tcg = mm_acc(bkg, ps["a"][:, fc % 2, :],
                         [(T["swg"][:, dc, fc * 128:(fc + 1) * 128], xg[:, dc, :])
                          for dc in range(DC)])
            bku = ("a", 2 + fc % 2)
            tcu = mm_acc(bku, ps["a"][:, 2 + fc % 2, :],
                         [(T["swu"][:, dc, fc * 128:(fc + 1) * 128], xg[:, dc, :])
                          for dc in range(DC)])
            Em.wait("scalar", "s_qa")
            evac("scalar", bkg, tcg, lambda e: e.activation(
                T["qe"][:], ps["a"][:, fc % 2, :], ACT.Silu))
            Em.wait("vector", "a_ev", Em.cnt["a_ev"])
            evac("vector", bku, tcu, lambda e, fc=fc: e.tensor_tensor(
                T["h"][:, fc, :], T["qe"][:], ps["a"][:, 2 + fc % 2, :], OP.mult))
        Em.wait("tensor", "s_ws", 48)
        for tt in range(4):
            for half in range(2):
                bk = ("b", 0)
                tc = mm_acc(bk, ps["b"][:, 0, :],
                            [(T["h"][:, fc, tt * 128:(tt + 1) * 128],
                              T["swd"][:, fc, half * 512:(half + 1) * 512])
                             for fc in range(FC)])
                hdst = T["qe"] if half == 0 else T["qe2"]
                sq = "s_qa" if half == 0 else "s_qb"
                Em.wait("scalar", sq)
                evac("scalar", bk, tc, lambda e, hdst=hdst: e.activation(
                    hdst[:], ps["b"][:, 0, :], ACT.Copy))
                Em.dma("scalar", sq, smv[g * 4 + tt][:, half * 512:(half + 1) * 512], hdst[:])
        Em.wait("gpsimd", "t_mm", Em.cnt["t_mm"])

    Em.barrier()
    if STOP_AFTER == "ph3":
        _dbg_stop(Em, ap, dr, T, dbg)
        return

    Em.dma("gpsimd", "s_wq", T["wqkv"][:], ap["wqkv"].rearrange("(c p) n -> p c n", p=128))
    Em.wait("gpsimd", "s_wq")
    Em.wait("gpsimd", "s_cc", cc_xn)
    Em.wait("gpsimd", "s_zr")
    for g in range(8):
        Em.op("gpsimd", "s_gf", lambda e, g=g: e.dma_gather(
            out_ap=T["xnf_o"][:], in_ap=dr["ag_out_xn"].ap(),
            idxs_ap=T["batch_idx"][:, 128 + g * 16:128 + (g + 1) * 16], num_idxs=256,
            num_idxs_reg=256, elem_size=D, transpose=True), 16)
        gf = Em.cnt["s_gf"]
        xg = T["xnf_o"]
        Em.wait("tensor", "s_gf", gf)
        Em.wait("tensor", "s_wq", 32)
        for kc in range(DC):
            bk = ("a", kc % 4)
            tc = mm_acc(bk, ps["a"][:, kc % 4, 0:256],
                        [(T["wqkv"][:, dc, D + kc * 128:D + (kc + 1) * 128], xg[:, dc, :])
                         for dc in range(DC)])
            evac("vector", bk, tc, lambda e, g=g, kc=kc: e.tensor_copy(
                T["K"][:, 1, kc, g * 256:(g + 1) * 256], ps["a"][:, kc % 4, 0:256]))
        for tt in range(2):
            for vc in range(2):
                bk = ("b", 0)
                tc = mm_acc(bk, ps["b"][:, 0, :],
                            [(xg[:, dc, tt * 128:(tt + 1) * 128],
                              T["wqkv"][:, dc, 2 * D + vc * 512:2 * D + (vc + 1) * 512])
                             for dc in range(DC)])
                evac("vector", bk, tc, lambda e, g=g, tt=tt, vc=vc: e.tensor_copy(
                    T["V"][:, 1, g * 2 + tt, vc * 512:(vc + 1) * 512], ps["b"][:, 0, :]))
        Em.wait("gpsimd", "t_mm", Em.cnt["t_mm"])

    Em.barrier()
    if STOP_AFTER == "ph4":
        _dbg_stop(Em, ap, dr, T, dbg)
        return

    _routing(Em, ap, dr, T, ps, mm_acc, evac, transpose, bank_free, cc_lg)
    if STOP_AFTER in ("r0z", "r0a", "r1", "r2", "r3", "r4", "r5", "r6", "r7"):
        return

    Em.barrier()

    Em.dma("gpsimd", "s_wo", T["outw_n"][:], ap["outw"].rearrange("(c p) n -> p c n", p=128))
    Em.wait("tensor", "s_wo")
    if STOP_AFTER == "ph5w":
        return
    for r in range(DC):
        for c in range(DC):
            tslot = (r * DC + c) % 2
            tc = transpose(("t", tslot), ps["t"][:, tslot, 0:128],
                           T["outw_n"][:, r, c * 128:(c + 1) * 128], T["ident"][:])
            evac("vector", ("t", tslot), tc, lambda e, r=r, c=c, tslot=tslot: e.tensor_copy(
                T["woT"][:, c, r * 128:(r + 1) * 128], ps["t"][:, tslot, 0:128]))
            if STOP_AFTER == "w1" and r == 0 and c == 0:
                return
            if STOP_AFTER == "w4" and r == 0 and c == 3:
                return
        if STOP_AFTER == "w64" and r == DC - 1:
            return

    Em.barrier()
    if STOP_AFTER == "ph5":
        _dbg_stop(Em, ap, dr, T, dbg)
        return

    _attention(Em, ap, dr, T, ps, mm_acc, evac, transpose, bank_free)

    Em.barrier()
    if STOP_AFTER == "ph6":
        _dbg_stop(Em, ap, dr, T, dbg)
        return

    _experts(Em, ap, dr, T, ps, mm_acc, evac, bank_free)

    Em.barrier()
    if STOP_AFTER == "ph7":
        _dbg_stop(Em, ap, dr, T, dbg)
        return

    _combine(Em, ap, dr, T, ps)

    if dbg:
        _debug_out(Em, ap, dr, T, dbg)
        Em.wait("sync", "s_dbg")

    Em.wait("sync", "s_out")


def _routing(Em, ap, dr, T, ps, mm_acc, evac, transpose, bank_free, cc_lg):
    if STOP_AFTER == "r0z":
        return
    Em.wait("sync", "s_cc", cc_lg)
    if STOP_AFTER == "r0a":
        return
    srcv = dr["ag_out_lg"].ap().rearrange("(cb j) ce -> j cb ce", j=64)
    lg3 = T["lg_pm"].ap()
    lgv = lg3.rearrange("p (cb c) e -> p cb (c e)", cb=8)
    Em.dma("sync", "s_lg", lgv[0:64, :, :], srcv)
    Em.dma("sync", "s_lg", lgv[64:128, :, :], srcv)
    Em.wait("vector", "s_lg")
    if STOP_AFTER == "r0b":
        return

    def b16(apx, n=NCH):
        return apx.unsqueeze(2).broadcast_to([128, n, E])

    Em.op("vector", "v_ev", lambda e: e.reduce_max(T["m1pm"][:], lg3, axis=AX.X))
    Em.op("vector", "v_ev", lambda e: e.tensor_tensor(
        T["oh1"][:], lg3, b16(T["m1pm"][:]), OP.is_equal))
    Em.op("vector", "v_ev", lambda e: e.scalar_tensor_tensor(
        lg3, T["oh1"][:], -1e30, lg3, OP.mult, OP.add))
    Em.op("vector", "v_ev", lambda e: e.reduce_max(T["m2pm"][:], lg3, axis=AX.X))
    Em.op("vector", "v_ev", lambda e: e.tensor_tensor(
        T["oh2"][:], lg3, b16(T["m2pm"][:]), OP.is_equal))
    Em.op("vector", "v_ev", lambda e: e.tensor_scalar(
        T["ohk"][:], T["oh1"][:], T["even_col"][:], None, OP.mult))
    Em.op("vector", "v_ev", lambda e: e.scalar_tensor_tensor(
        T["ohk"][:], T["oh2"][:], T["odd_col"][:], T["ohk"][:], OP.mult, OP.add))
    vv = Em.cnt["v_ev"]
    if STOP_AFTER == "r1":
        return

    ohf = T["ohk"].ap().rearrange("p c e -> p (c e)")
    exf = T["excl"].ap().rearrange("p c e -> p (c e)")
    for j in range(8):
        bk = ("a", j % 4)
        tc = mm_acc(bk, ps["a"][:, j % 4, :],
                    [(T["lts"][:], ohf[:, j * 512:(j + 1) * 512])], waits=[("v_ev", vv)])
        evac("vector", bk, tc, lambda e, j=j: e.tensor_copy(
            exf[:, j * 512:(j + 1) * 512], ps["a"][:, j % 4, :]))
        bk2 = ("b", 0)
        tc2 = mm_acc(bk2, ps["b"][0:1, 0, :],
                     [(T["ones_col"][:], ohf[:, j * 512:(j + 1) * 512])],
                     waits=[("v_ev", vv)])
        evac("vector", bk2, tc2, lambda e, j=j: e.tensor_copy(
            T["trow"][0:1, j * 512:(j + 1) * 512], ps["b"][0:1, 0, :]))

    trow3 = T["trow"].ap().rearrange("a (c e) -> a c e", e=E)
    brow3 = T["brow"].ap().rearrange("a (c e) -> a c e", e=E)
    for ee in range(E):
        Em.op("vector", "v_ev", lambda e, ee=ee: e.tensor_tensor_scan(
            brow3[:, :, ee], trow3[:, :, ee], trow3[:, :, ee], 0.0, OP.add, OP.bypass))
    Em.op("vector", "v_ev", lambda e: e.tensor_tensor(
        T["brow"][:], T["brow"][:], T["trow"][:], OP.subtract))
    vb = Em.cnt["v_ev"]
    if STOP_AFTER == "r2":
        return

    for eloc in range(2):
        ec = T["ec0"] if eloc == 0 else T["ec1"]
        berow = T["be_row0"] if eloc == 0 else T["be_row1"]
        for hh in range(2):
            Em.op("vector", "v_ev", lambda e, hh=hh, ec=ec: e.tensor_tensor(
                T["scrH"][0:1, :, :], brow3[:, hh * 128:(hh + 1) * 128, :],
                ec.ap()[0:1, :].unsqueeze(1).broadcast_to([1, 128, E]), OP.mult))
            Em.op("vector", "v_ev", lambda e, hh=hh, berow=berow: e.reduce_sum(
                berow[0:1, hh * 128:(hh + 1) * 128], T["scrH"][0:1, :, :], axis=AX.X))

    Em.op("vector", "v_ev", lambda e: e.memset(T["pos_pair"][:], 0.0))
    for hh in range(2):
        sl = slice(hh * 128, (hh + 1) * 128)
        Em.op("vector", "v_ev", lambda e, sl=sl: e.tensor_tensor(
            T["scrH"][:], T["ohk"][:, sl, :], T["excl"][:, sl, :], OP.mult))
        Em.op("vector", "v_ev", lambda e, sl=sl: e.reduce_sum(
            T["pos_pair"][:, sl], T["scrH"][:], axis=AX.X))
    for ee in range(E):
        bk = ("c",)
        tc = mm_acc(bk, ps["c"][:, 0:NCH],
                    [(T["ones_row"][:], brow3[:, :, ee])], waits=[("v_ev", vb)])
        evac("vector", bk, tc, lambda e: e.tensor_copy(T["bbc"][:], ps["c"][:, 0:NCH]))
        Em.op("vector", "v_ev", lambda e, ee=ee: e.tensor_tensor(
            T["bbc"][:], T["ohk"][:, :, ee], T["bbc"][:], OP.mult))
        Em.op("vector", "v_ev", lambda e: e.tensor_add(
            T["pos_pair"][:], T["pos_pair"][:], T["bbc"][:]))
    vpp = Em.cnt["v_ev"]

    for k in range(2):
        selk = T["sel_k0"] if k == 0 else T["sel_k1"]
        for cc in range(2):
            bk = ("c",)
            tc = mm_acc(bk, ps["c"][:, 0:64],
                        [(T["pos_pair"][:, cc * 128:(cc + 1) * 128], selk[:])],
                        waits=[("v_ev", vpp)])
            evac("vector", bk, tc, lambda e, cc=cc: e.tensor_copy(
                T["posTj"][:, cc, :], ps["c"][:, 0:64]))
        vtj = Em.cnt["v_ev"]
        for par in range(2):
            mc = T["m_even"] if par == 0 else T["m_odd"]
            bk = ("c",)
            tc = mm_acc(bk, ps["c"][0:64, 0:16],
                        [(T["posTj"][:, cc, :], mc[:, cc, :]) for cc in range(2)],
                        waits=[("v_ev", vtj)])
            evac("vector", bk, tc, lambda e: e.tensor_copy(
                T["stg"][0:64, 0:16], ps["c"][0:64, 0:16]))
            vstg = Em.cnt["v_ev"]
            tc2 = transpose(("c",), ps["c"][0:16, 0:64], T["stg"][0:64, 0:16],
                            T["identf"][0:64, 0:64], waits=[("v_ev", vstg)])
            evac("vector", ("c",), tc2, lambda e, par=par: e.tensor_copy(
                T["stg2"][0:16, par * 64:(par + 1) * 64], ps["c"][0:16, 0:64]))
        vst = Em.cnt["v_ev"]
        tc3 = transpose(("c",), ps["c"][:, 0:16], T["stg2"][0:16, 0:128],
                        T["identf"][0:16, 0:16], waits=[("v_ev", vst)])
        evac("vector", ("c",), tc3, lambda e, k=k: e.tensor_copy(
            T["pos0" if k == 0 else "pos1"][:], ps["c"][:, 0:16]))
    if STOP_AFTER == "r3":
        return

    for eloc in range(2):
        ec = T["ec0"] if eloc == 0 else T["ec1"]
        member = T["member0"] if eloc == 0 else T["member1"]
        pose = T["pose0"] if eloc == 0 else T["pose1"]
        for hh in range(2):
            sl = slice(hh * 128, (hh + 1) * 128)
            Em.op("vector", "v_ev", lambda e, sl=sl, ec=ec: e.tensor_tensor(
                T["scrH"][:], T["ohk"][:, sl, :],
                ec.ap().unsqueeze(1).broadcast_to([128, 128, E]), OP.mult))
            Em.op("vector", "v_ev", lambda e, sl=sl, member=member: e.reduce_sum(
                member[:, sl], T["scrH"][:], axis=AX.X))
            Em.op("vector", "v_ev", lambda e, sl=sl, ec=ec: e.tensor_tensor(
                T["scrH"][:], T["excl"][:, sl, :],
                ec.ap().unsqueeze(1).broadcast_to([128, 128, E]), OP.mult))
            Em.op("vector", "v_ev", lambda e, sl=sl, pose=pose: e.reduce_sum(
                pose[:, sl], T["scrH"][:], axis=AX.X))

    lgo = T["lg_own"].ap()

    def bo(apx):
        return apx.unsqueeze(2).broadcast_to([128, 16, E])

    iE = T["iota_row"][:, 0:16].unsqueeze(1).broadcast_to([128, 16, E])
    s5 = T["scrH"].ap().rearrange("p c e -> p (c e)")
    Em.op("vector", "v_ev", lambda e: e.reduce_max(T["m1t"][:], lgo, axis=AX.X))
    Em.op("vector", "v_ev", lambda e: e.tensor_tensor(
        s5[:, 0:256].rearrange("p (c e) -> p c e", e=E), lgo, bo(T["m1t"][:]), OP.is_equal))
    Em.op("vector", "v_ev", lambda e: e.tensor_tensor(
        s5[:, 256:512].rearrange("p (c e) -> p c e", e=E),
        s5[:, 0:256].rearrange("p (c e) -> p c e", e=E), iE, OP.mult))
    Em.op("vector", "v_ev", lambda e: e.reduce_sum(
        T["e1t"][:], s5[:, 256:512].rearrange("p (c e) -> p c e", e=E), axis=AX.X))
    Em.op("vector", "v_ev", lambda e: e.tensor_tensor(
        s5[:, 256:512].rearrange("p (c e) -> p c e", e=E),
        s5[:, 0:256].rearrange("p (c e) -> p c e", e=E),
        T["parrow"].ap().unsqueeze(1).broadcast_to([128, 16, E]), OP.mult))
    Em.op("vector", "v_ev", lambda e: e.reduce_sum(
        T["par1"][:], s5[:, 256:512].rearrange("p (c e) -> p c e", e=E), axis=AX.X))
    Em.op("vector", "v_ev", lambda e: e.tensor_tensor(
        s5[:, 256:512].rearrange("p (c e) -> p c e", e=E),
        s5[:, 0:256].rearrange("p (c e) -> p c e", e=E),
        T["halfrow"].ap().unsqueeze(1).broadcast_to([128, 16, E]), OP.mult))
    Em.op("vector", "v_ev", lambda e: e.reduce_sum(
        T["rb1"][:], s5[:, 256:512].rearrange("p (c e) -> p c e", e=E), axis=AX.X))
    Em.op("vector", "v_ev", lambda e: e.scalar_tensor_tensor(
        s5[:, 512:768].rearrange("p (c e) -> p c e", e=E),
        s5[:, 0:256].rearrange("p (c e) -> p c e", e=E), -1e30, lgo, OP.mult, OP.add))
    Em.op("vector", "v_ev", lambda e: e.reduce_max(
        T["m2t"][:], s5[:, 512:768].rearrange("p (c e) -> p c e", e=E), axis=AX.X))
    Em.op("vector", "v_ev", lambda e: e.tensor_tensor(
        s5[:, 768:1024].rearrange("p (c e) -> p c e", e=E),
        s5[:, 512:768].rearrange("p (c e) -> p c e", e=E), bo(T["m2t"][:]), OP.is_equal))
    Em.op("vector", "v_ev", lambda e: e.tensor_tensor(
        s5[:, 1024:1280].rearrange("p (c e) -> p c e", e=E),
        s5[:, 768:1024].rearrange("p (c e) -> p c e", e=E), iE, OP.mult))
    Em.op("vector", "v_ev", lambda e: e.reduce_sum(
        T["e2t"][:], s5[:, 1024:1280].rearrange("p (c e) -> p c e", e=E), axis=AX.X))
    Em.op("vector", "v_ev", lambda e: e.tensor_tensor(
        s5[:, 1024:1280].rearrange("p (c e) -> p c e", e=E),
        s5[:, 768:1024].rearrange("p (c e) -> p c e", e=E),
        T["parrow"].ap().unsqueeze(1).broadcast_to([128, 16, E]), OP.mult))
    Em.op("vector", "v_ev", lambda e: e.reduce_sum(
        T["par2"][:], s5[:, 1024:1280].rearrange("p (c e) -> p c e", e=E), axis=AX.X))
    Em.op("vector", "v_ev", lambda e: e.tensor_tensor(
        s5[:, 1024:1280].rearrange("p (c e) -> p c e", e=E),
        s5[:, 768:1024].rearrange("p (c e) -> p c e", e=E),
        T["halfrow"].ap().unsqueeze(1).broadcast_to([128, 16, E]), OP.mult))
    Em.op("vector", "v_ev", lambda e: e.reduce_sum(
        T["rb2"][:], s5[:, 1024:1280].rearrange("p (c e) -> p c e", e=E), axis=AX.X))
    Em.op("vector", "v_ev", lambda e: e.tensor_tensor(
        T["scr16"][:], T["m2t"][:], T["m1t"][:], OP.subtract))
    Em.op("vector", "v_ev", lambda e: e.tensor_tensor(
        s5[:, 1024:1280].rearrange("p (c e) -> p c e", e=E), lgo, bo(T["m1t"][:]), OP.subtract))
    vexp = Em.cnt["v_ev"]
    Em.wait("scalar", "v_ev", vexp)
    Em.run("scalar", lambda e: e.preload_act_table(ACT.Exp))
    Em.op("scalar", "a_ev", lambda e: e.activation(T["scr16b"][:], T["scr16"][:], ACT.Exp))
    Em.op("scalar", "a_ev", lambda e: e.activation(
        s5[:, 1280:1536], s5[:, 1024:1280], ACT.Exp))
    aexp = Em.cnt["a_ev"]
    Em.wait("vector", "a_ev", aexp)
    Em.op("vector", "v_ev", lambda e: e.reduce_sum(
        T["zt"][:], s5[:, 1280:1536].rearrange("p (c e) -> p c e", e=E), axis=AX.X))
    Em.op("vector", "v_ev", lambda e: e.tensor_scalar(
        T["dent"][:], T["zt"][:], 1e-9, 1.0, OP.mult, OP.add))
    Em.op("vector", "v_ev", lambda e: e.tensor_add(T["dent"][:], T["dent"][:], T["scr16b"][:]))
    Em.op("vector", "v_ev", lambda e: e.reciprocal(T["w1t"][:], T["dent"][:]))
    Em.op("vector", "v_ev", lambda e: e.tensor_tensor(
        T["w2t"][:], T["scr16b"][:], T["w1t"][:], OP.mult))

    for k in range(2):
        et = T["e1t"] if k == 0 else T["e2t"]
        post = T["pos0"] if k == 0 else T["pos1"]
        wt = T["w1t"] if k == 0 else T["w2t"]
        wA = T["wA0" if k == 0 else "wA1"]
        wB = T["wB0" if k == 0 else "wB1"]
        iA = T["iA0" if k == 0 else "iA1"]
        iB = T["iB0" if k == 0 else "iB1"]
        win = T["win0" if k == 0 else "win1"]
        par = T["par1"] if k == 0 else T["par2"]
        rbase = T["rb1"] if k == 0 else T["rb2"]
        Em.op("vector", "v_ev", lambda e, post=post, win=win: e.tensor_scalar(
            win[:], post[:], float(CAP), None, OP.is_lt))
        Em.op("vector", "v_ev", lambda e, post=post, iA=iA, rbase=rbase: e.tensor_tensor(
            iA[:], rbase[:], post[:], OP.add))
        Em.op("vector", "v_ev", lambda e, iA=iA: e.tensor_scalar(
            iA[:], iA[:], float(NCORE * CAP - 1), 0.0, OP.min, OP.max))
        Em.op("vector", "v_ev", lambda e, iB=iB, iA=iA: e.tensor_copy(iB[:], iA[:]))
        Em.op("vector", "v_ev", lambda e, wt=wt, win=win: e.tensor_tensor(
            T["dent"][:], wt[:], win[:], OP.mult))
        Em.op("vector", "v_ev", lambda e, wB=wB, par=par: e.tensor_tensor(
            wB[:], T["dent"][:], par[:], OP.mult))
        Em.op("vector", "v_ev", lambda e, wA=wA, wB=wB: e.tensor_tensor(
            wA[:], T["dent"][:], wB[:], OP.subtract))
    vgi = Em.cnt["v_ev"]
    if STOP_AFTER == "r4":
        return

    idxp8 = T["idxpre"].ap().rearrange("p (r c) -> p r c", c=8)
    for nm_src, nm_dst in [("iA0", "cA0"), ("iB0", "cB0"), ("iA1", "cA1"), ("iB1", "cB1")]:
        for cc in range(8):
            bk = ("c",)
            tc = mm_acc(bk, ps["c"][0:16, 0:16],
                        [(T["identf"][:, cc * 16:(cc + 1) * 16], T[nm_src][:])],
                        waits=[("v_ev", vgi)])
            evac("vector", bk, tc, lambda e, cc=cc: e.tensor_copy(
                idxp8[0:16, 0:16, cc], ps["c"][0:16, 0:16]))
        Em.op("vector", "v_ev", lambda e, nm_dst=nm_dst: e.tensor_copy(
            T[nm_dst][0:16, :], T["idxpre"][0:16, 0:128]))
        vrep = Em.cnt["v_ev"]
        Em.wait("sync", "v_ev", vrep)
        for r in range(1, 8):
            Em.dma("sync", "s_rep", T[nm_dst][16 * r:16 * (r + 1), :], T[nm_dst][0:16, :])

    if STOP_AFTER == "r5":
        return
    last_oh_tmm = {}
    for eloc in range(2):
        member = T["member0"] if eloc == 0 else T["member1"]
        pose = T["pose0"] if eloc == 0 else T["pose1"]
        berow = T["be_row0"] if eloc == 0 else T["be_row1"]
        bk = ("c",)
        tc = mm_acc(bk, ps["c"][:, 0:NCH], [(T["ones_row"][:], berow[0:1, :])],
                    waits=[("v_ev", Em.cnt["v_ev"])])
        evac("vector", bk, tc, lambda e: e.tensor_copy(T["bbc"][:], ps["c"][:, 0:NCH]))
        Em.op("vector", "v_ev", lambda e, pose=pose: e.tensor_add(
            pose[:], pose[:], T["bbc"][:]))
        Em.op("vector", "v_ev", lambda e, pose=pose: e.tensor_scalar(
            T["val"][:], pose[:], float(SLOTPAD), None, OP.is_lt))
        Em.op("vector", "v_ev", lambda e, member=member: e.tensor_tensor(
            T["val"][:], T["val"][:], member[:], OP.mult))
        Em.op("vector", "v_ev", lambda e: e.scalar_tensor_tensor(
            T["val"][:], T["tokbase"][:], 1.0, T["val"][:], OP.add, OP.mult))
        for qq in range(4):
            sl4 = slice(qq * 64, (qq + 1) * 64)
            Em.op("vector", "v_ev", lambda e, sl4=sl4, pose=pose: e.tensor_tensor(
                T["scrH"].ap().rearrange("p c e -> p (c e)")[:, 0:64 * ROWS]
                 .rearrange("p (c r) -> p c r", r=ROWS),
                pose[:, sl4].unsqueeze(2).broadcast_to([128, 64, ROWS]),
                T["thresh"].ap().unsqueeze(1).broadcast_to([128, 64, ROWS]), OP.is_ge))
            Em.op("vector", "v_ev", lambda e, sl4=sl4: e.reduce_sum(
                T["rowi"][:, sl4],
                T["scrH"].ap().rearrange("p c e -> p (c e)")[:, 0:64 * ROWS]
                 .rearrange("p (c r) -> p c r", r=ROWS), axis=AX.X))
        Em.op("vector", "v_ev", lambda e, pose=pose: e.scalar_tensor_tensor(
            T["posm"][:], T["rowi"][:], -128.0, pose[:], OP.mult, OP.add))
        abank = ("a", eloc)
        if abank in bank_free:
            Em.wait("tensor", bank_free[abank][0], bank_free[abank][1])
        sbank = ("b", 0)
        if sbank in bank_free:
            Em.wait("tensor", bank_free[sbank][0], bank_free[sbank][1])
        for cq in range(64):
            slot = cq % 4
            if slot in last_oh_tmm:
                Em.wait("vector", "t_mm", last_oh_tmm[slot])
            Em.op("vector", "v_ev", lambda e, cq=cq, slot=slot: e.tensor_tensor(
                T["ohsl"][:, slot, :, :],
                T["iota_row"][:, 0:128].unsqueeze(1).broadcast_to([128, 4, 128]),
                T["posm"][:, cq * 4:(cq + 1) * 4].unsqueeze(2).broadcast_to([128, 4, 128]),
                OP.is_equal))
            Em.op("vector", "v_ev", lambda e, cq=cq, slot=slot: e.tensor_tensor(
                T["scrH"].ap().rearrange("p c e -> p (c e)")[:, slot * 128:slot * 128 + 4 * ROWS]
                 .rearrange("p (c r) -> p c r", r=ROWS),
                T["rowi"][:, cq * 4:(cq + 1) * 4].unsqueeze(2).broadcast_to([128, 4, ROWS]),
                T["iota_row"][:, 0:ROWS].unsqueeze(1).broadcast_to([128, 4, ROWS]),
                OP.is_equal))
            Em.op("vector", "v_ev", lambda e, cq=cq, slot=slot: e.tensor_tensor(
                T["scrH"].ap().rearrange("p c e -> p (c e)")[:, slot * 128:slot * 128 + 4 * ROWS]
                 .rearrange("p (c r) -> p c r", r=ROWS),
                T["scrH"].ap().rearrange("p c e -> p (c e)")[:, slot * 128:slot * 128 + 4 * ROWS]
                 .rearrange("p (c r) -> p c r", r=ROWS),
                T["val"][:, cq * 4:(cq + 1) * 4].unsqueeze(2).broadcast_to([128, 4, ROWS]),
                OP.mult))
            voh = Em.cnt["v_ev"]
            Em.wait("tensor", "v_ev", voh)
            for j in range(4):
                ch = cq * 4 + j
                Em.op("tensor", "t_mm", lambda e, slot=slot, j=j, ch=ch: e.matmul(
                    ps["b"][:, 0, 0:ROWS],
                    T["ohsl"][:, slot, j, :],
                    T["scrH"].ap().rearrange("p c e -> p (c e)")[:, slot * 128 + j * ROWS:slot * 128 + (j + 1) * ROWS],
                    start=(ch == 0), stop=(ch == NCH - 1), skip_group_check=True))
            last_oh_tmm[slot] = Em.cnt["t_mm"]
        tred = Em.cnt["t_mm"]
        evac("vector", sbank, tred, lambda e: e.tensor_copy(
            T["sAT"][:, 0:ROWS], ps["b"][:, 0, 0:ROWS]))
        vsat = Em.cnt["v_ev"]
        idxp8b = T["idxpre"].ap().rearrange("p (r c) -> p r c", c=8)
        for cc in range(8):
            bk = ("c",)
            tc = mm_acc(bk, ps["c"][0:16, 0:ROWS],
                        [(T["identf"][:, cc * 16:(cc + 1) * 16], T["sAT"][:, 0:ROWS])],
                        waits=[("v_ev", vsat)])
            evac("vector", bk, tc, lambda e, cc=cc: e.tensor_copy(
                idxp8b[0:16, 0:ROWS, cc], ps["c"][0:16, 0:ROWS]))
        eidx_t = T["eidx0"] if eloc == 0 else T["eidx1"]
        Em.op("vector", "v_ev", lambda e, eidx_t=eidx_t: e.tensor_copy(
            eidx_t[0:16, :], T["idxpre"][0:16, 0:160]))
        vei = Em.cnt["v_ev"]
        Em.wait("sync", "v_ev", vei)
        for r in range(1, 8):
            Em.dma("sync", "s_rep", eidx_t[16 * r:16 * (r + 1), :], eidx_t[0:16, :])
        if STOP_AFTER == "r6" and eloc == 0:
            return



def _dbg_stop(Em, ap, dr, T, dbg):
    if dbg:
        _debug_out(Em, ap, dr, T, dbg)
        Em.wait("sync", "s_dbg")

def _attention(Em, ap, dr, T, ps, mm_acc, evac, transpose, bank_free):
    Em.run("scalar", lambda e: e.preload_act_table(ACT.Exp))
    Em.wait("sync", "s_qa")
    Em.wait("sync", "s_qb")
    qsv = dr["q_scr"].ap()
    asv = dr["attn_scr"].ap().rearrange("(i p) d -> i p d", p=128)
    for g in range(8):
        for sub in range(2):
            qt = g * 2 + sub
            Em.wait("sync", "t_mm", Em.cnt["t_mm"])
            Em.dma("sync", "s_qt", T["Qt"][:],
                   qsv[:, :, qt * 128:(qt + 1) * 128].rearrange("c p t -> p c t"))
            Em.wait("tensor", "s_qt")
            for kc in range(8):
                half, kk = divmod(kc, 4)
                bk = ("a", kc % 4)
                tc = mm_acc(bk, ps["a"][:, kc % 4, :],
                            [(T["Qt"][:, dc, :], T["K"][:, half, dc, kk * 512:(kk + 1) * 512])
                             for dc in range(DC)])
                if kc % 2 == 0:
                    evac("vector", bk, tc, lambda e, kc=kc: e.tensor_copy(
                        T["S"][:, kc * 512:(kc + 1) * 512], ps["a"][:, kc % 4, :]))
                else:
                    evac("scalar", bk, tc, lambda e, kc=kc: e.activation(
                        T["S"][:, kc * 512:(kc + 1) * 512], ps["a"][:, kc % 4, :], ACT.Copy))
            Em.wait("vector", "a_ev", Em.cnt["a_ev"])
            Em.op("vector", "v_ev", lambda e, qt=qt: e.reduce_max(
                T["attn_m"][:, qt:qt + 1], T["S"][:], axis=AX.X, negate=True))
            vmax = Em.cnt["v_ev"]
            Em.wait("scalar", "v_ev", vmax)
            Em.op("scalar", "a_ev", lambda e, qt=qt: e.activation(
                T["P"][:], T["S"][:], ACT.Exp, bias=T["attn_m"][:, qt:qt + 1],
                accum_out=T["attn_l"][:, qt:qt + 1]))
            aexp = Em.cnt["a_ev"]
            for kt in range(32):
                tslot = kt % 2
                tc = transpose(("t", tslot), ps["t"][:, tslot, 0:128],
                               T["P"][:, kt * 128:(kt + 1) * 128], T["ident"][:],
                               waits=[("a_ev", aexp)])
                if kt % 2 == 0:
                    evac("vector", ("t", tslot), tc, lambda e, kt=kt, sub=sub, tslot=tslot:
                         e.tensor_copy(T["PT"][:, kt, sub * 128:(sub + 1) * 128],
                                       ps["t"][:, tslot, 0:128]))
                else:
                    evac("scalar", ("t", tslot), tc, lambda e, kt=kt, sub=sub, tslot=tslot:
                         e.activation(T["PT"][:, kt, sub * 128:(sub + 1) * 128],
                                      ps["t"][:, tslot, 0:128], ACT.Copy))
        vpt = Em.cnt["v_ev"]
        apt = Em.cnt["a_ev"]
        for dc in range(DC):
            bk = ("a", 2 + dc % 2)
            pairs = []
            for kt in range(32):
                half, tt = divmod(kt, 16)
                pairs.append((T["V"][:, half, tt, dc * 128:(dc + 1) * 128], T["PT"][:, kt, :]))
            tc = mm_acc(bk, ps["a"][:, 2 + dc % 2, 0:256], pairs,
                        waits=[("v_ev", vpt), ("a_ev", apt)])
            evac("vector", bk, tc, lambda e, dc=dc: e.tensor_copy(
                T["Osb"][:, dc, :], ps["a"][:, 2 + dc % 2, 0:256]))
        Em.op("vector", "v_ev", lambda e, g=g: e.reciprocal(
            T["linv"][:, g * 2:g * 2 + 2], T["attn_l"][:, g * 2:g * 2 + 2]))
        vo = Em.cnt["v_ev"]
        for sub in range(2):
            qt = g * 2 + sub
            for oc in range(2):
                bk = ("a", oc)
                tc = mm_acc(bk, ps["a"][:, oc, :],
                            [(T["Osb"][:, dc, sub * 128:(sub + 1) * 128],
                              T["woT"][:, dc, oc * 512:(oc + 1) * 512])
                             for dc in range(DC)], waits=[("v_ev", vo)])
                qdst = T["qe"] if oc == 0 else T["qe2"]
                sq = "s_qa" if oc == 0 else "s_qb"
                Em.wait("scalar", sq)
                evac("scalar", bk, tc, lambda e, qt=qt, qdst=qdst, oc=oc: e.activation(
                    qdst[:], ps["a"][:, oc, :], ACT.Copy,
                    scale=T["linv"][:, qt:qt + 1]))
                Em.dma("scalar", sq, asv[qt][:, oc * 512:(oc + 1) * 512], qdst[:])


def _experts(Em, ap, dr, T, ps, mm_acc, evac, bank_free):
    Em.run("scalar", lambda e: e.preload_act_table(ACT.Silu))
    Em.op("vector", "v_ev", lambda e: e.memset(T["einp"][:], 0.0))
    vz = Em.cnt["v_ev"]
    for eloc in range(2):
        Em.dma("gpsimd", "s_we", T["ewg"][:],
               ap["wg"][eloc].rearrange("(c p) n -> p c n", p=128))
        Em.wait("gpsimd", "s_we")
        Em.dma("gpsimd", "s_we", T["ewu"][:],
               ap["wu"][eloc].rearrange("(c p) n -> p c n", p=128))
        Em.wait("gpsimd", "s_we")
        Em.dma("gpsimd", "s_we", T["ewd"][:],
               ap["wd"][eloc].rearrange("(c p) n -> p c n", p=128))
        Em.wait("gpsimd", "s_we")
        we_cnt = Em.cnt["s_we"]
        eidx_t = T["eidx0"] if eloc == 0 else T["eidx1"]
        Em.wait("gpsimd", "v_ev", vz)
        Em.wait("gpsimd", "t_mm", Em.cnt["t_mm"])
        Em.wait("gpsimd", "s_zr")
        Em.wait("gpsimd", "s_rep")
        Em.wait("gpsimd", "s_cc", 1)
        for qf in range(4):
            Em.op("gpsimd", "s_gf", lambda e, eidx_t=eidx_t, qf=qf: e.dma_gather(
                out_ap=T["einp"][:, qf, :, :],
                in_ap=dr["ag_out_xn"].ap(),
                idxs_ap=eidx_t[:, qf * 40:(qf + 1) * 40], num_idxs=640,
                num_idxs_reg=640, elem_size=D, transpose=True), 16)
            Em.wait("gpsimd", "s_gf")
        gf = Em.cnt["s_gf"]
        eov = (dr["ag_in_eoA"] if eloc == 0 else dr["ag_in_eoB"]).ap() \
            .rearrange("(i p) d -> i p d", p=128)
        Em.wait("tensor", "s_gf", gf)
        Em.wait("tensor", "s_we", we_cnt)
        chunks = [(q, scl, csz) for q in range(4) for (scl, csz) in [(0, 512), (512, 128)]]
        for sc, (hf, scl, csz) in enumerate(chunks):
            xg = T["einp"][:, hf, :, scl:scl + csz]
            for fc in range(FC):
                bkg = ("a", fc % 2)
                tcg = mm_acc(bkg, ps["a"][:, fc % 2, 0:csz],
                             [(T["ewg"][:, dc, fc * 128:(fc + 1) * 128], xg[:, dc, :])
                              for dc in range(DC)])
                bku = ("a", 2 + fc % 2)
                tcu = mm_acc(bku, ps["a"][:, 2 + fc % 2, 0:csz],
                             [(T["ewu"][:, dc, fc * 128:(fc + 1) * 128], xg[:, dc, :])
                              for dc in range(DC)])
                Em.wait("scalar", "s_qa")
                evac("scalar", bkg, tcg, lambda e, csz=csz, fc=fc: e.activation(
                    T["qe"][:, 0:csz], ps["a"][:, fc % 2, 0:csz], ACT.Silu))
                Em.wait("vector", "a_ev", Em.cnt["a_ev"])
                evac("vector", bku, tcu, lambda e, fc=fc, csz=csz: e.tensor_tensor(
                    T["eh"][:, fc, 0:csz], T["qe"][:, 0:csz],
                    ps["a"][:, 2 + fc % 2, 0:csz], OP.mult))
            for tt in range(csz // 128):
                Em.wait("vector", "s_qa" if tt % 2 == 0 else "s_qb")
                for half in range(2):
                    bk = ("b", 0)
                    tc = mm_acc(bk, ps["b"][:, 0, :],
                                [(T["eh"][:, fc, tt * 128:(tt + 1) * 128],
                                  T["ewd"][:, fc, half * 512:(half + 1) * 512])
                                 for fc in range(FC)])
                    evac("vector", bk, tc, lambda e, tt=tt, half=half: e.tensor_copy(
                        T["eo_t"][:, tt % 2, half * 512:(half + 1) * 512],
                        ps["b"][:, 0, :]))
                Em.wait("scalar", "v_ev", Em.cnt["v_ev"])
                rowt = (hf * 640 + scl) // 128 + tt
                Em.dma("scalar", "s_qa" if tt % 2 == 0 else "s_qb",
                       eov[rowt], T["eo_t"][:, tt % 2, :])
        Em.wait("gpsimd", "s_qa")
        Em.wait("gpsimd", "s_qb")
        Em.op("gpsimd", "s_cc", lambda e, eloc=eloc: e.collective_compute(
            "AllGather", OP.bypass, replica_groups=[list(range(NCORE))],
            ins=[(dr["ag_in_eoA"] if eloc == 0 else dr["ag_in_eoB"]).ap().opt()],
            outs=[(dr["ag_out_eoA"] if eloc == 0 else dr["ag_out_eoB"]).ap().opt()]))


def _combine(Em, ap, dr, T, ps):
    Em.wait("gpsimd", "s_cc")
    for nm_idx, nm_buf, src in [("cA0", "gA0", "ag_out_eoA"), ("cB0", "gB0", "ag_out_eoB"),
                                ("cA1", "gA1", "ag_out_eoA"), ("cB1", "gB1", "ag_out_eoB")]:
        for hf in range(2):
            Em.op("gpsimd", "s_gf", lambda e, nm_idx=nm_idx, nm_buf=nm_buf, src=src, hf=hf:
                  e.dma_gather(
                      out_ap=T[nm_buf][:, hf * 8:(hf + 1) * 8, :], in_ap=dr[src].ap(),
                      idxs_ap=T[nm_idx][:, hf * 64:(hf + 1) * 64], num_idxs=NT // 2,
                      num_idxs_reg=NT // 2, elem_size=D, transpose=False), 16)
            Em.wait("gpsimd", "s_gf")
    gf = Em.cnt["s_gf"]
    Em.wait("sync", "s_qa")
    Em.wait("sync", "s_qb")
    Em.dma("sync", "s_ob", T["outb"][:], ap["outb"])
    xv = ap["x"].rearrange("(i p) d -> i p d", p=128)
    asv = dr["attn_scr"].ap().rearrange("(i p) d -> i p d", p=128)
    smv = dr["smlp_scr"].ap().rearrange("(i p) d -> i p d", p=128)
    ov = ap["out"].rearrange("(i p) d -> i p d", p=128)
    Em.wait("vector", "s_gf", gf)
    Em.wait("vector", "s_ob")
    for i in range(16):
        sx = "s_xa" if i % 2 == 0 else "s_xb"
        Em.dma("sync", sx, T["cx"][:, i % 2, :], xv[i])
        Em.dma("sync", sx, T["ca"][:, i % 2, :], asv[i])
        Em.dma("sync", sx, T["cs"][:, i % 2, :], smv[i])
        Em.wait("vector", sx)

        def comb(e, i=i):
            e.tensor_scalar(T["acc"][:], T["cx"][:, i % 2, :],
                            T["isg1"][:, i:i + 1], T["nms"][:, i:i + 1], OP.mult, OP.add)
            e.tensor_add(T["acc"][:], T["acc"][:], T["outb"][:])
            e.tensor_add(T["acc"][:], T["acc"][:], T["ca"][:, i % 2, :])
            e.tensor_add(T["acc"][:], T["acc"][:], T["cs"][:, i % 2, :])
            e.scalar_tensor_tensor(T["acc"][:], T["gA0"][:, i, :], T["wA0"][:, i:i + 1],
                                   T["acc"][:], OP.mult, OP.add)
            e.scalar_tensor_tensor(T["acc"][:], T["gB0"][:, i, :], T["wB0"][:, i:i + 1],
                                   T["acc"][:], OP.mult, OP.add)
            e.scalar_tensor_tensor(T["acc"][:], T["gA1"][:, i, :], T["wA1"][:, i:i + 1],
                                   T["acc"][:], OP.mult, OP.add)
            return e.scalar_tensor_tensor(T["acc"][:], T["gB1"][:, i, :], T["wB1"][:, i:i + 1],
                                          T["acc"][:], OP.mult, OP.add)
        Em.op("vector", "v_ev", comb)
        Em.wait("sync", "v_ev")
        Em.dma("sync", "s_out", ov[i], T["acc"][:])


def _debug_out(Em, ap, dr, T, dbg):
    def d2d(dst, src):
        Em.dma("sync", "s_dbg", dst, src)
    d2d(dbg["lg"], dr["ag_out_lg"].ap())
    d2d(dbg["q"], dr["q_scr"].ap())
    d2d(dbg["attn"], dr["attn_scr"].ap())
    d2d(dbg["smlp"], dr["smlp_scr"].ap())
    d2d(dbg["eoA"], dr["ag_in_eoA"].ap())
    d2d(dbg["eoB"], dr["ag_in_eoB"].ap())
    d2d(dbg["K"], T["K"].ap().rearrange("p a c t -> p (a c t)"))
    d2d(dbg["V"], T["V"].ap().rearrange("p a c t -> p (a c t)"))
    d2d(dbg["einp"], T["einp"][:, 0, :, :].rearrange("p c t -> p (c t)"))
    for j, nm in enumerate(["mu", "isg", "st2", "st3", "nms", "isg1"]):
        d2d(dbg["stats"][:, j, :], T[nm][:])
    d2d(dbg["pos"][:, 0, :], T["pos0"][:])
    d2d(dbg["pos"][:, 1, :], T["pos1"][:])
    for j, nm in enumerate(["wA0", "wB0", "wA1", "wB1", "iA0", "iB0", "iA1", "iB1",
                            "e1t", "e2t", "w1t", "w2t"]):
        d2d(dbg["wm"][:, j, :], T[nm][:])
    d2d(dbg["eidx"][:, 0, :], T["eidx0"][:])
    d2d(dbg["eidx"][:, 1, :], T["eidx1"][:])



_prog_cache = {}


def kernel(L0_attn_x, L0_attn_weight_qkv, L0_attn_out_w, L0_attn_out_b,
           L0_ffn_router_w, L0_ffn_router_b, L0_ffn_w_gate, L0_ffn_w_up,
           L0_ffn_w_down, L0_ffn_sw_gate, L0_ffn_sw_up, L0_ffn_sw_down):
    x = np.ascontiguousarray(np.asarray(L0_attn_x, dtype=np.float32)).reshape(N, D)
    wqkv = np.ascontiguousarray(np.asarray(L0_attn_weight_qkv, np.float32))
    outw = np.ascontiguousarray(np.asarray(L0_attn_out_w, np.float32))
    outb = np.asarray(L0_attn_out_b, np.float32)
    rw = np.ascontiguousarray(np.asarray(L0_ffn_router_w, np.float32))
    rb = np.asarray(L0_ffn_router_b, np.float32)
    wg = np.ascontiguousarray(np.asarray(L0_ffn_w_gate, np.float32))
    wu = np.ascontiguousarray(np.asarray(L0_ffn_w_up, np.float32))
    wd = np.ascontiguousarray(np.asarray(L0_ffn_w_down, np.float32))
    swg = np.ascontiguousarray(np.asarray(L0_ffn_sw_gate, np.float32))
    swu = np.ascontiguousarray(np.asarray(L0_ffn_sw_up, np.float32))
    swd = np.ascontiguousarray(np.asarray(L0_ffn_sw_down, np.float32))

    if "nc" not in _prog_cache:
        _prog_cache["nc"] = build_program()
    nc = _prog_cache["nc"]

    outb_bc = np.ascontiguousarray(np.tile(outb[None, :], (128, 1)))
    rb_bc = np.ascontiguousarray(np.tile(rb[None, :], (128, 1)))

    in_maps = []
    for c in range(NCORE):
        consts = build_consts(c)
        m = {
            "x_own": np.ascontiguousarray(x[c * NT:(c + 1) * NT]),
            "w_qkv": wqkv, "out_w": outw, "out_b_bc": outb_bc,
            "router_w": rw, "router_b_bc": rb_bc,
            "wg_pair": np.ascontiguousarray(wg[2 * c:2 * c + 2]),
            "wu_pair": np.ascontiguousarray(wu[2 * c:2 * c + 2]),
            "wd_pair": np.ascontiguousarray(wd[2 * c:2 * c + 2]),
            "sw_gate": swg, "sw_up": swu, "sw_down": swd,
        }
        for k in CONST_SHAPES:
            m["c_" + k] = np.ascontiguousarray(consts[k])
        m["c_m_even"] = np.ascontiguousarray(consts["m_even"])
        m["c_m_odd"] = np.ascontiguousarray(consts["m_odd"])
        in_maps.append(m)

    res = run_bass_kernel_spmd(nc, in_maps, core_ids=list(range(NCORE)))
    out = np.concatenate([res.results[c]["out"] for c in range(NCORE)], axis=0)
    if DBG:
        kernel._dbg = res.results
    return out.reshape(4, 4096, D)


# version 05 (final)
# speedup vs baseline: 1.0167x; 1.0167x over previous
import numpy as np
import ml_dtypes

import concourse.bacc as bacc
import concourse.mybir as mybir
from concourse.bass_utils import run_bass_kernel_spmd

F32 = mybir.dt.float32
BF16 = mybir.dt.bfloat16
I16 = mybir.dt.int16
OP = mybir.AluOpType
AX = mybir.AxisListType
ACT = mybir.ActivationFunctionType
BF = ml_dtypes.bfloat16

NCORE = 8
D = 1024
DC = 8
NT = 2048
NB = 4096
N = 16384
E = 16
F = 1408
FC = 11
CAP = 2560
ROWS = 21
SLOTPAD = ROWS * 128
NCH = 256
EPS = 1e-5
KB = 1024

DBG = False
STOP_AFTER = "full"

ENGS = ("sync", "scalar", "vector", "tensor", "gpsimd")


class HazardGuard:

    RW = {
        "tensor_tensor": ([0], [1, 2]),
        "tensor_copy": ([0], [1]),
        "tensor_scalar": ([0], [1, 2, 3]),
        "reduce_sum": ([0], [1]),
        "reduce_max": ([0], [1]),
        "scalar_tensor_tensor": ([0], [1, 2, 3]),
        "tensor_add": ([0], [1, 2]),
        "reciprocal": ([0], [1]),
        "memset": ([0], []),
        "tensor_tensor_scan": ([0], [1, 2, 3]),
        "activation": ([0], [1, 3, 4]),
    }

    def __init__(self, eng, is_scalar, junk):
        self._eng = eng
        self._sc = is_scalar
        self._junk = junk
        self._w1 = frozenset()
        self._w2 = frozenset()
        self._slot = 0
        self.n_spacers = 0

    @staticmethod
    def _names(objs):
        out = set()
        for a in objs:
            t = getattr(a, "tensor", None)
            if t is not None:
                out.add(t.name)
            elif hasattr(a, "name") and hasattr(a, "shape"):
                out.add(a.name)
        return out

    def _spacer(self):
        j = self._junk
        for _ in range(2):
            if self._sc:
                self._eng.activation(j[:, 1, :], j[:, 2, :], ACT.Copy)
            else:
                s = self._slot
                self._slot ^= 1
                self._eng.tensor_scalar(j[:, s, :], j[:, 3, :], 1.0, None, OP.mult)
            self.n_spacers += 1
        self._w2 = self._w1 = frozenset(("__junk__",))

    def __getattr__(self, nm):
        fn = getattr(self._eng, nm)
        if nm == "dma_start":
            def dwrap(*args, **kw):
                rd = self._names(list(args[1:2]) + [kw.get("in_")])
                if rd & (self._w1 | self._w2):
                    self._spacer()
                return fn(*args, **kw)
            return dwrap
        spec = self.RW.get(nm)
        if spec is None:
            return fn
        widx, ridx = spec
        def wrapped(*args, **kw):
            reads = [args[i] for i in ridx if i < len(args)]
            for k in ("bias", "scale", "in1", "d0", "d1"):
                if kw.get(k) is not None:
                    reads.append(kw[k])
            writes = [args[i] for i in widx if i < len(args)]
            if kw.get("out") is not None:
                writes.append(kw["out"])
            if kw.get("accum_out") is not None:
                writes.append(kw["accum_out"])
            rn = self._names(reads)
            if rn & (self._w1 | self._w2):
                self._spacer()
            ins = fn(*args, **kw)
            self._w2 = self._w1
            self._w1 = frozenset(self._names(writes))
            return ins
        return wrapped


def _sp_tile(vals):
    n = len(vals)
    t = np.zeros((16, n // 16), np.int16)
    t[np.arange(n) % 16, np.arange(n) // 16] = np.asarray(vals, np.int64)
    return np.tile(t, (8, 1))


def build_consts(core):
    p = np.arange(128)
    c = {}
    c["ident"] = np.eye(128, dtype=BF)
    c["identf"] = np.eye(128, dtype=np.float32)
    order = 2 * (p % 64) + p // 64
    c["lts"] = (order[:, None] < order[None, :]).astype(BF)
    c["ones_col"] = np.ones((128, 1), dtype=BF)
    c["ones_row"] = np.ones((1, 128), np.float32)
    s0 = np.zeros((128, 64), np.float32)
    s1 = np.zeros((128, 64), np.float32)
    s0[np.arange(64), np.arange(64)] = 1.0
    s1[64 + np.arange(64), np.arange(64)] = 1.0
    c["sel_k0"], c["sel_k1"] = s0, s1
    me = np.zeros((NCH, 16), np.float32)
    mo = np.zeros((NCH, 16), np.float32)
    for u in range(32):
        (me if u % 2 == 0 else mo)[32 * core + u, u // 2] = 1.0
    c["m_even"], c["m_odd"] = me, mo
    c["iota_row"] = np.tile(np.arange(512, dtype=np.float32)[None, :], (128, 1))
    c["tokbase"] = (np.arange(NCH)[None, :] * 64 + (p % 64)[:, None]).astype(np.float32)
    c["even_col"] = (p[:, None] < 64).astype(np.float32)
    c["odd_col"] = (p[:, None] >= 64).astype(np.float32)
    c["own_idx"] = _sp_tile(np.arange(NT))
    b, h = core // 2, core % 2
    own_g = np.arange(b * NB + h * NT, b * NB + (h + 1) * NT)
    par_g = np.arange(b * NB + (1 - h) * NT, b * NB + (2 - h) * NT)
    c["batch_idx"] = _sp_tile(np.concatenate([own_g, par_g]) + 1)
    c["parrow"] = np.tile((np.arange(16) % 2).astype(np.float32)[None, :], (128, 1))
    c["halfrow"] = np.tile(((np.arange(16) // 2) * CAP).astype(np.float32)[None, :], (128, 1))
    c["thresh"] = np.tile((128.0 * (1 + np.arange(ROWS))).astype(np.float32)[None, :], (128, 1))
    e0 = np.zeros((128, 16), np.float32); e0[:, 2 * core] = 1.0
    e1 = np.zeros((128, 16), np.float32); e1[:, 2 * core + 1] = 1.0
    c["ec0"], c["ec1"] = e0, e1
    return c


CONST_SHAPES = {
    "ident": ([128, 128], BF16), "identf": ([128, 128], F32),
    "lts": ([128, 128], BF16),
    "ones_col": ([128, 1], BF16), "ones_row": ([1, 128], F32),
    "sel_k0": ([128, 64], F32), "sel_k1": ([128, 64], F32),
    "iota_row": ([128, 512], F32),
    "tokbase": ([128, NCH], F32),
    "even_col": ([128, 1], F32), "odd_col": ([128, 1], F32),
    "own_idx": ([128, 128], I16), "batch_idx": ([128, 256], I16),
    "ec0": ([128, 16], F32), "ec1": ([128, 16], F32),
    "parrow": ([128, 16], F32), "halfrow": ([128, 16], F32),
    "thresh": ([128, ROWS], F32),
}


class Emit:

    def __init__(self, name, eng, sems):
        self.name = name
        self.eng = eng
        self.sems = sems
        self.cnt = {k: 0 for k in sems}
        self.barno = 0

    def wait(self, owner, sem, val=None):
        v = self.cnt[sem] if val is None else val
        if self.name == owner and v > 0:
            self.eng.wait_ge(self.sems[sem], v)

    def dma(self, owner, sem, out, in_, **kw):
        self.cnt[sem] += 16
        if self.name == owner:
            self.eng.dma_start(out=out, in_=in_, **kw).then_inc(self.sems[sem], 16)
        return self.cnt[sem]

    def op(self, owner, sem, fn, n=1):
        if sem is not None:
            self.cnt[sem] += n
        if self.name == owner:
            ins = fn(self.eng)
            if sem is not None and ins is not None:
                ins.then_inc(self.sems[sem], n)
        return self.cnt[sem] if sem is not None else None

    def run(self, owner, fn):
        if self.name == owner:
            fn(self.eng)

    def barrier(self):
        self.barno += 1
        self.eng.sem_inc(self.sems["b_" + self.name], 1)
        for other in ENGS:
            if other != self.name:
                self.eng.wait_ge(self.sems["b_" + other], self.barno)


def build_program():
    nc = bacc.Bacc("TRN2", target_bir_lowering=False, debug=False,
                   detect_race_conditions=False)

    ap = {}
    ap["x"] = nc.dram_tensor("x_own", [NT, D], F32, kind="ExternalInput").ap()
    ap["wqkv"] = nc.dram_tensor("w_qkv", [D, 3 * D], F32, kind="ExternalInput").ap()
    ap["outw"] = nc.dram_tensor("out_w", [D, D], F32, kind="ExternalInput").ap()
    ap["outb"] = nc.dram_tensor("out_b_bc", [128, D], F32, kind="ExternalInput").ap()
    ap["rw"] = nc.dram_tensor("router_w", [D, E], F32, kind="ExternalInput").ap()
    ap["rb"] = nc.dram_tensor("router_b_bc", [128, E], F32, kind="ExternalInput").ap()
    ap["wg"] = nc.dram_tensor("wg_pair", [2, D, F], F32, kind="ExternalInput").ap()
    ap["wu"] = nc.dram_tensor("wu_pair", [2, D, F], F32, kind="ExternalInput").ap()
    ap["wd"] = nc.dram_tensor("wd_pair", [2, F, D], F32, kind="ExternalInput").ap()
    ap["swg"] = nc.dram_tensor("sw_gate", [D, F], F32, kind="ExternalInput").ap()
    ap["swu"] = nc.dram_tensor("sw_up", [D, F], F32, kind="ExternalInput").ap()
    ap["swd"] = nc.dram_tensor("sw_down", [F, D], F32, kind="ExternalInput").ap()
    for k, (shp, dt) in CONST_SHAPES.items():
        ap[k] = nc.dram_tensor("c_" + k, shp, dt, kind="ExternalInput").ap()
    ap["m_even"] = nc.dram_tensor("c_m_even", [NCH, 16], F32, kind="ExternalInput").ap()
    ap["m_odd"] = nc.dram_tensor("c_m_odd", [NCH, 16], F32, kind="ExternalInput").ap()
    ap["out"] = nc.dram_tensor("out", [NT, D], F32, kind="ExternalOutput").ap()

    dr = {}
    dr["ag_in_xn"] = nc.dram_tensor("ag_in_xn", [NT, D], BF16)
    dr["ag_out_xn"] = nc.dram_tensor("ag_out_xn", [N + 1, D], BF16)
    dr["ag_in_lg"] = nc.dram_tensor("ag_in_lg", [64, 32 * E], F32)
    dr["ag_out_lg"] = nc.dram_tensor("ag_out_lg", [512, 32 * E], F32)
    dr["ag_in_eoA"] = nc.dram_tensor("ag_in_eoA", [CAP, D], BF16)
    dr["ag_out_eoA"] = nc.dram_tensor("ag_out_eoA", [NCORE * CAP, D], BF16)
    dr["ag_in_eoB"] = nc.dram_tensor("ag_in_eoB", [CAP, D], BF16)
    dr["ag_out_eoB"] = nc.dram_tensor("ag_out_eoB", [NCORE * CAP, D], BF16)
    dr["q_scr"] = nc.dram_tensor("q_scr", [DC, 128, NT], BF16)
    dr["attn_scr"] = nc.dram_tensor("attn_scr", [NT, D], BF16)
    dr["smlp_scr"] = nc.dram_tensor("smlp_scr", [NT, D], BF16)

    dbg = {}
    if DBG:
        def dout(name, shp, dt=F32):
            dbg[name] = nc.dram_tensor("dbg_" + name, shp, dt, kind="ExternalOutput").ap()
        dout("stats", [128, 6, 16])
        dout("lg", [512, 32 * E])
        dout("pos", [128, 2, 16])
        dout("slots", [128, 2, 128])
        dout("wm", [128, 12, 16])
        dout("eidx", [128, 2, 160], I16)
        dout("q", [DC, 128, NT], BF16)
        dout("attn", [NT, D], BF16)
        dout("smlp", [NT, D], BF16)
        dout("eoA", [CAP, D], BF16)
        dout("eoB", [CAP, D], BF16)
        dout("K", [128, 2 * DC * NT], BF16)
        dout("V", [128, 32 * D], BF16)
        dout("einp", [128, DC * 640], BF16)

    sb = {}
    arena_bytes = (nc.sbuf_bytes_remaining - 2048) // 32 * 32
    nc.alloc_sbuf_tensor("arena", [128, arena_bytes], mybir.dt.int8)
    arena_base = None
    for alloc in nc.m.functions[0].allocations:
        if isinstance(alloc, mybir.MemoryLocationSet) and alloc.memorylocations:
            ml = alloc.memorylocations[0]
            if ml.name.startswith("arena"):
                arena_base = ml.addr
    assert arena_base is not None

    def at(name, off, shape, dtype):
        t = nc.alloc_sbuf_tensor_at(name, shape, dtype, offset=arena_base + off)
        sb[name] = t
        return t

    o = [0]

    def pat(name, shape, dtype, nbytes):
        t = at(name, o[0], shape, dtype)
        o[0] += (nbytes + 31) // 32 * 32
        return t

    pat("ident", [128, 128], BF16, 256)
    pat("identf", [128, 128], F32, 512)
    pat("lts", [128, 128], BF16, 256)
    pat("ones_col", [128, 1], BF16, 32)
    pat("ones_row", [1, 128], F32, 512)
    pat("sel_k0", [128, 64], F32, 256)
    pat("sel_k1", [128, 64], F32, 256)
    pat("m_even", [128, 2, 16], F32, 128)
    pat("m_odd", [128, 2, 16], F32, 128)
    pat("iota_row", [128, 512], F32, 2048)
    pat("tokbase", [128, NCH], F32, 1024)
    pat("even_col", [128, 1], F32, 32)
    pat("odd_col", [128, 1], F32, 32)
    pat("own_idx", [128, 128], I16, 256)
    pat("batch_idx", [128, 256], I16, 512)
    pat("ec0", [128, 16], F32, 64)
    pat("ec1", [128, 16], F32, 64)
    pat("parrow", [128, 16], F32, 64)
    pat("halfrow", [128, 16], F32, 64)
    pat("thresh", [128, ROWS], F32, 128)
    pat("rb", [128, E], F32, 64)
    pat("rwb", [128, DC, E], BF16, 256)
    for nm in ["mu", "isg", "st1", "st2", "st3", "nms", "isg1",
               "m1t", "m2t", "e1t", "e2t", "w1t", "w2t", "zt", "dent",
               "wA0", "wB0", "wA1", "wB1", "iA0", "iB0", "iA1", "iB1",
               "pos0", "pos1", "attn_m", "attn_l", "linv",
               "scr16", "scr16b", "win0", "win1",
               "par1", "par2", "rb1", "rb2"]:
        pat(nm, [128, 16], F32, 64)
    pat("lg_own", [128, 16, E], F32, 1024)
    pat("eidx0", [128, 160], I16, 320)
    pat("eidx1", [128, 160], I16, 320)
    for nm in ["cA0", "cB0", "cA1", "cB1"]:
        pat(nm, [128, 128], I16, 256)
    pat("qe", [128, 512], BF16, 1024)
    pat("qe2", [128, 512], BF16, 1024)
    pat("junk", [128, 4, 4], F32, 64)

    WX = (o[0] + 127) // 128 * 128
    RK = WX + 16 * KB
    RV = RK + 64 * KB
    RC = RV + 64 * KB
    assert RC + 48 * KB <= arena_bytes, (RC, arena_bytes)

    at("xt", WX, [128, 2, D], F32)
    at("sqscr", WX + 8 * KB, [128, 2, D], BF16)
    at("xnt", WX + 12 * KB, [128, 512], BF16)
    at("zero_row", WX + 13 * KB, [1, D], BF16)
    at("xnf", WX, [128, 2, DC, 512], BF16)
    at("xnf_o", WX, [128, DC, 256], BF16)
    at("ohk", WX, [128, NCH, E], BF16)
    at("member0", WX + 8 * KB, [128, NCH], F32)
    at("member1", WX + 9 * KB, [128, NCH], F32)
    at("runs", WX + 10 * KB, [128, 512], F32)
    at("runsT", WX + 12 * KB, [128, 4, 128], F32)
    at("woT", WX, [128, DC, D], BF16)

    at("K", RK, [128, 2, DC, NT], BF16)
    at("V", RV, [128, 2, 16, D], BF16)
    at("swg", RK + 32 * KB, [128, DC, F], BF16)
    at("swu", RV + 32 * KB, [128, DC, F], BF16)
    at("einp", RK, [128, 4, DC, CAP // 4], BF16)
    at("ewd", RV, [128, FC, D], BF16)
    at("eh", RV + 22 * KB, [128, FC, 512], BF16)
    at("eo_t", RV + 33 * KB, [128, 2, D], BF16)
    at("gA0", RK, [128, 16, D], BF16)
    at("gB0", RK + 32 * KB, [128, 16, D], BF16)
    at("gA1", RV, [128, 16, D], BF16)
    at("gB1", RV + 32 * KB, [128, 16, D], BF16)

    at("wqkv", RC, [128, DC, 3 * D], BF16)
    at("swd", RC, [128, FC, D], BF16)
    at("h", RC + 22 * KB, [128, FC, 512], BF16)
    at("outw_n", RC, [128, DC, D], BF16)
    at("lg_pm", RC, [128, NCH, E], F32)
    at("oh1", RC + 16 * KB, [128, NCH, E], BF16)
    at("oh2", RC + 24 * KB, [128, NCH, E], BF16)
    at("excl", RC, [128, NCH, E], BF16)
    at("scrROW", RC, [1, NCH * ROWS], F32)
    at("scrH", RC + 8 * KB, [128, 128, E], F32)
    at("trow", RC + 16 * KB, [1, 4096], F32)
    at("brow", RC + 32 * KB, [1, 4096], F32)
    at("bbc", RC + 16 * KB, [128, NCH], F32)
    at("slotA", RC + 17 * KB, [128, 128], F32)
    at("slotB", RC + 17 * KB + 512, [128, 128], F32)
    at("idxpre", RC + 18 * KB, [128, 168], F32)
    at("bbc2", RC + 19 * KB, [128, NCH], F32)
    at("rA_row", RC + 20 * KB, [1, 256], F32)
    at("stg", RC + 21 * KB, [128, 128], F32)
    at("stg2", RC + 21 * KB + 512, [128, 128], F32)
    at("posTj", RC + 22 * KB, [128, 2, 64], F32)
    at("rhs2", RC + 43 * KB, [128, NCH, 2], F32)
    at("ohsl", RC + 25 * KB, [128, 4, 4, 128], F32)
    at("rv_row", RC + 33 * KB, [1, 512], F32)
    at("pose0", RC + 35 * KB, [128, NCH], F32)
    at("pose1", RC + 36 * KB, [128, NCH], F32)
    at("posm", RC + 37 * KB, [128, NCH], F32)
    at("rowi", RC + 38 * KB, [128, NCH], F32)
    at("val", RC + 39 * KB, [128, NCH], F32)
    at("rvv", RC + 40 * KB, [128, 4], F32)
    at("Rq", RC + 40 * KB + 32, [128, 32], F32)
    at("sAT", RC + 40 * KB + 160, [128, 32], F32)
    at("pos_pair", RC + 25 * KB, [128, NCH], F32)
    at("m1pm", RC + 32 * KB, [128, NCH], F32)
    at("m2pm", RC + 33 * KB, [128, NCH], F32)
    at("be_row0", RC + 23 * KB, [1, 256], F32)
    at("be_row1", RC + 24 * KB, [1, 256], F32)
    at("S", RC, [128, NB], F32)
    at("P", RC + 16 * KB, [128, NB], BF16)
    at("PT", RC + 24 * KB, [128, 32, 256], BF16)
    at("Qt", RC + 40 * KB, [128, DC, 128], BF16)
    at("Osb", RC + 42 * KB, [128, DC, 256], BF16)
    at("Qt2", RC + 46 * KB, [128, DC, 128], BF16)
    at("ewg", RC, [128, DC, F], BF16)
    at("ewu", RC + 22 * KB, [128, DC, F], BF16)
    at("acc", RC, [128, D], F32)
    at("cx", RC + 4 * KB, [128, 2, D], F32)
    at("ca", RC + 12 * KB, [128, 2, D], BF16)
    at("cs", RC + 16 * KB, [128, 2, D], BF16)
    at("outb", RC + 20 * KB, [128, D], F32)

    ps = {}
    ps["a"] = nc.alloc_psum_tensor("ps_a", [128, 4, 512], F32)
    ps["b"] = nc.alloc_psum_tensor("ps_b", [128, 1, 512], F32)
    ps["c"] = nc.alloc_psum_tensor("ps_c", [128, 512], F32)
    ps["t"] = nc.alloc_psum_tensor("ps_t", [128, 2, 1024], BF16)

    sem_names = ["s_xa", "s_xb", "s_lg", "s_qt", "s_ob",
                 "s_wq", "s_ws", "s_we", "s_wo", "s_cst", "s_cc",
                 "s_gf", "s_qa", "s_qb", "s_qx", "s_pay", "s_zr", "s_out",
                 "s_dbg", "s_rep",
                 "t_mm", "v_ev", "a_ev",
                 "b_sync", "b_scalar", "b_vector", "b_tensor", "b_gpsimd"]

    import contextlib
    stack = contextlib.ExitStack()
    sems = {s: stack.enter_context(nc.semaphore(s)) for s in sem_names}
    block = stack.enter_context(nc.Block())

    def emit_for(name):
        def f(eng):
            _program(Emit(name, eng, sems), ap, dr, sb, ps, dbg)
        return f

    block.sync(emit_for("sync"))
    block.scalar(emit_for("scalar"))
    block.vector(emit_for("vector"))
    block.tensor(emit_for("tensor"))
    block.gpsimd(emit_for("gpsimd"))

    stack.close()
    nc.compile()
    return nc



def _program(Em, ap, dr, sb, ps, dbg):
    T = sb
    if Em.name == "vector":
        Em.eng = HazardGuard(Em.eng, False, T["junk"])
        Em.eng._eng.memset(T["junk"][:, 3, :], 1.0)
    elif Em.name == "scalar":
        Em.eng = HazardGuard(Em.eng, True, T["junk"])
        Em.eng._eng.memzero(T["junk"][:, 2, :])
    bank_free = {}

    def mm_acc(bank_key, bank_ap, pairs, waits=()):
        if bank_key in bank_free:
            sem, cv = bank_free[bank_key]
            Em.wait("tensor", sem, cv)
        for w in waits:
            Em.wait("tensor", w[0], w[1])
        nmm = len(pairs)
        for i, (l, r) in enumerate(pairs):
            last = i == nmm - 1
            Em.op("tensor", "t_mm" if last else None,
                  lambda e, l=l, r=r, i=i, last=last: e.matmul(
                      bank_ap, l, r, start=(i == 0), stop=last,
                      skip_group_check=True))
        return Em.cnt["t_mm"]

    def evac(owner, bank_key, tcnt, fn):
        sem = "v_ev" if owner == "vector" else "a_ev"
        if tcnt is not None:
            Em.wait(owner, "t_mm", tcnt)
        c = Em.op(owner, sem, fn)
        if bank_key is not None:
            bank_free[bank_key] = (sem, c)
        return c

    def transpose(bank_key, out_ap, in_ap, ident_ap, waits=()):
        if bank_key in bank_free:
            sem, cv = bank_free[bank_key]
            Em.wait("tensor", sem, cv)
        for w in waits:
            Em.wait("tensor", w[0], w[1])
        return Em.op("tensor", "t_mm",
                     lambda e: e.transpose(out_ap, in_ap, ident_ap))

    for k in CONST_SHAPES:
        Em.dma("sync", "s_cst", T[k][:], ap[k])
    Em.dma("sync", "s_cst", T["m_even"][:], ap["m_even"].rearrange("(a p) e -> p a e", p=128))
    Em.dma("sync", "s_cst", T["m_odd"][:], ap["m_odd"].rearrange("(a p) e -> p a e", p=128))
    Em.dma("sync", "s_cst", T["rb"][:], ap["rb"])
    Em.wait("tensor", "s_cst")
    Em.wait("vector", "s_cst")

    Em.dma("gpsimd", "s_wq", T["wqkv"][:], ap["wqkv"].rearrange("(c p) n -> p c n", p=128))
    Em.wait("gpsimd", "s_wq")
    Em.dma("gpsimd", "s_ws", T["swg"][:], ap["swg"].rearrange("(c p) n -> p c n", p=128))
    Em.wait("gpsimd", "s_ws")
    Em.dma("gpsimd", "s_ws", T["swu"][:], ap["swu"].rearrange("(c p) n -> p c n", p=128))
    Em.wait("gpsimd", "s_ws")
    Em.dma("gpsimd", "s_wo", T["rwb"][:], ap["rw"].rearrange("(c p) e -> p c e", p=128))

    Em.op("vector", "v_ev", lambda e: e.memset(T["zero_row"][:], 0.0))
    Em.wait("gpsimd", "v_ev")
    Em.dma("gpsimd", "s_zr", dr["ag_out_xn"].ap()[0:1, :], T["zero_row"][:])

    Em.run("scalar", lambda e: e.preload_act_table(ACT.Sqrt))

    xv = ap["x"].rearrange("(i p) d -> i p d", p=128)
    xnv = dr["ag_in_xn"].ap().rearrange("(i p) d -> i p d", p=128)
    for i in range(16):
        sx = "s_xa" if i % 2 == 0 else "s_xb"
        Em.dma("sync", sx, T["xt"][:, i % 2, :], xv[i])
        Em.wait("scalar", sx)
        Em.op("scalar", "a_ev", lambda e, i=i: e.activation(
            T["sqscr"][:, i % 2, :], T["xt"][:, i % 2, :], ACT.Square,
            accum_out=T["st2"][:, i:i + 1]))
        Em.wait("vector", sx)
        Em.op("vector", "v_ev", lambda e, i=i: e.reduce_sum(
            T["st1"][:, i:i + 1], T["xt"][:, i % 2, :], axis=AX.X))
        Em.wait("sync", "a_ev", Em.cnt["a_ev"] - 1)
        Em.wait("sync", "v_ev", Em.cnt["v_ev"] - 1)
    Em.wait("vector", "a_ev")

    def _stats(e):
        e.tensor_scalar(T["mu"][:], T["st1"][:], 1.0 / D, None, OP.mult)
        e.tensor_scalar(T["st3"][:], T["st2"][:], 1.0 / D, None, OP.mult)
        e.tensor_tensor(T["st1"][:], T["mu"][:], T["mu"][:], OP.mult)
        e.tensor_tensor(T["st3"][:], T["st3"][:], T["st1"][:], OP.subtract)
        e.tensor_scalar(T["st3"][:], T["st3"][:], EPS, None, OP.add)
        return e.reciprocal(T["st2"][:], T["st3"][:])
    Em.op("vector", "v_ev", _stats)
    Em.wait("scalar", "v_ev")
    Em.op("scalar", "a_ev", lambda e: e.activation(T["isg"][:], T["st2"][:], ACT.Sqrt))
    Em.wait("vector", "a_ev")

    def _newton(e):
        e.tensor_tensor(T["st1"][:], T["isg"][:], T["isg"][:], OP.mult)
        e.tensor_tensor(T["st1"][:], T["st1"][:], T["st3"][:], OP.mult)
        e.tensor_scalar(T["st1"][:], T["st1"][:], -0.5, 1.5, OP.mult, OP.add)
        e.tensor_tensor(T["isg"][:], T["isg"][:], T["st1"][:], OP.mult)
        e.tensor_tensor(T["nms"][:], T["mu"][:], T["isg"][:], OP.mult)
        e.tensor_scalar(T["nms"][:], T["nms"][:], -1.0, None, OP.mult)
        return e.tensor_scalar(T["isg1"][:], T["isg"][:], 1.0, 1.0, OP.mult, OP.add)
    Em.op("vector", "v_ev", _newton)

    for i in range(16):
        sx = "s_xa" if i % 2 == 0 else "s_xb"
        Em.dma("sync", sx, T["xt"][:, i % 2, :], xv[i])
        for hh in range(2):
            Em.wait("vector", sx)
            Em.wait("vector", "s_qx")
            Em.op("vector", "v_ev", lambda e, i=i, hh=hh: e.tensor_scalar(
                T["xnt"][:, :], T["xt"][:, i % 2, hh * 512:(hh + 1) * 512],
                T["isg"][:, i:i + 1], T["nms"][:, i:i + 1], OP.mult, OP.add))
            Em.wait("scalar", "v_ev")
            Em.dma("scalar", "s_qx", xnv[i][:, hh * 512:(hh + 1) * 512], T["xnt"][:, :])
        Em.wait("sync", "v_ev", Em.cnt["v_ev"] - 2)
    Em.wait("gpsimd", "s_qx")
    Em.op("gpsimd", "s_cc", lambda e: e.collective_compute(
        "AllGather", OP.bypass, replica_groups=[list(range(NCORE))],
        ins=[dr["ag_in_xn"].ap().opt()],
        outs=[dr["ag_out_xn"].ap()[1:N + 1, :].opt()]))
    cc_xn = Em.cnt["s_cc"]

    Em.barrier()
    if STOP_AFTER == "ph1":
        _dbg_stop(Em, ap, dr, T, dbg)
        return

    qsv = dr["q_scr"].ap()
    for g in range(4):
        Em.op("gpsimd", "s_gf", lambda e, g=g: e.dma_gather(
            out_ap=T["xnf"][:, g % 2, :, :], in_ap=dr["ag_in_xn"].ap(),
            idxs_ap=T["own_idx"][:, g * 32:(g + 1) * 32], num_idxs=512,
            num_idxs_reg=512, elem_size=D, transpose=True), 16)
        gf = Em.cnt["s_gf"]
        xg = T["xnf"][:, g % 2]
        Em.wait("tensor", "s_gf", gf)
        Em.wait("tensor", "s_wq", 16)
        Em.wait("tensor", "s_wo", 16)
        for tt in range(4):
            tc = mm_acc(("c",), ps["c"][:, 0:E],
                        [(xg[:, dc, tt * 128:(tt + 1) * 128], T["rwb"][:, dc, :])
                         for dc in range(DC)])
            evac("vector", ("c",), tc, lambda e, g=g, tt=tt: e.tensor_add(
                T["lg_own"][:, g * 4 + tt, :], ps["c"][:, 0:E], T["rb"][:]))
        for qc in range(DC):
            bk = ("a", qc % 4)
            tc = mm_acc(bk, ps["a"][:, qc % 4, :],
                        [(T["wqkv"][:, dc, qc * 128:(qc + 1) * 128], xg[:, dc, :])
                         for dc in range(DC)])
            qdst = T["qe"] if qc % 2 == 0 else T["qe2"]
            sq = "s_qa" if qc % 2 == 0 else "s_qb"
            Em.wait("scalar", sq)
            evac("scalar", bk, tc, lambda e, qdst=qdst: e.activation(
                qdst[:], ps["a"][:, qc % 4, :], ACT.Copy, scale=1.0 / 32.0))
            Em.dma("scalar", sq, qsv[qc, :, g * 512:(g + 1) * 512], qdst[:])
        for kc in range(DC):
            bk = ("a", kc % 4)
            tc = mm_acc(bk, ps["a"][:, kc % 4, :],
                        [(T["wqkv"][:, dc, D + kc * 128:D + (kc + 1) * 128], xg[:, dc, :])
                         for dc in range(DC)])
            evac("vector", bk, tc, lambda e, g=g, kc=kc: e.tensor_copy(
                T["K"][:, 0, kc, g * 512:(g + 1) * 512], ps["a"][:, kc % 4, :]))
        for tt in range(4):
            for vc in range(2):
                bk = ("b", 0)
                tc = mm_acc(bk, ps["b"][:, 0, :],
                            [(xg[:, dc, tt * 128:(tt + 1) * 128],
                              T["wqkv"][:, dc, 2 * D + vc * 512:2 * D + (vc + 1) * 512])
                             for dc in range(DC)])
                evac("vector", bk, tc, lambda e, g=g, tt=tt, vc=vc: e.tensor_copy(
                    T["V"][:, 0, g * 4 + tt, vc * 512:(vc + 1) * 512], ps["b"][:, 0, :]))
        Em.wait("gpsimd", "t_mm", Em.cnt["t_mm"])

    pay = dr["ag_in_lg"].ap().rearrange("j (c e) -> j c e", e=E)
    Em.wait("sync", "v_ev")
    Em.dma("sync", "s_pay", pay[:, 0::2, :], T["lg_own"][0:64, :, :])
    Em.dma("sync", "s_pay", pay[:, 1::2, :], T["lg_own"][64:128, :, :])
    Em.wait("gpsimd", "s_pay")
    Em.op("gpsimd", "s_cc", lambda e: e.collective_compute(
        "AllGather", OP.bypass, replica_groups=[list(range(NCORE))],
        ins=[dr["ag_in_lg"].ap().opt()],
        outs=[dr["ag_out_lg"].ap().opt()]))
    cc_lg = Em.cnt["s_cc"]

    Em.barrier()
    if STOP_AFTER == "ph2":
        _dbg_stop(Em, ap, dr, T, dbg)
        return

    Em.run("scalar", lambda e: e.preload_act_table(ACT.Silu))
    Em.dma("gpsimd", "s_ws", T["swd"][:], ap["swd"].rearrange("(c p) n -> p c n", p=128))
    Em.wait("gpsimd", "s_ws")
    smv = dr["smlp_scr"].ap().rearrange("(i p) d -> i p d", p=128)
    for g in range(4):
        Em.op("gpsimd", "s_gf", lambda e, g=g: e.dma_gather(
            out_ap=T["xnf"][:, g % 2, :, :], in_ap=dr["ag_in_xn"].ap(),
            idxs_ap=T["own_idx"][:, g * 32:(g + 1) * 32], num_idxs=512,
            num_idxs_reg=512, elem_size=D, transpose=True), 16)
        gf = Em.cnt["s_gf"]
        xg = T["xnf"][:, g % 2]
        Em.wait("tensor", "s_gf", gf)
        Em.wait("tensor", "s_ws", 32)
        for fc in range(FC):
            bkg = ("a", fc % 2)
            tcg = mm_acc(bkg, ps["a"][:, fc % 2, :],
                         [(T["swg"][:, dc, fc * 128:(fc + 1) * 128], xg[:, dc, :])
                          for dc in range(DC)])
            bku = ("a", 2 + fc % 2)
            tcu = mm_acc(bku, ps["a"][:, 2 + fc % 2, :],
                         [(T["swu"][:, dc, fc * 128:(fc + 1) * 128], xg[:, dc, :])
                          for dc in range(DC)])
            Em.wait("scalar", "s_qa")
            evac("scalar", bkg, tcg, lambda e: e.activation(
                T["qe"][:], ps["a"][:, fc % 2, :], ACT.Silu))
            Em.wait("vector", "a_ev", Em.cnt["a_ev"])
            evac("vector", bku, tcu, lambda e, fc=fc: e.tensor_tensor(
                T["h"][:, fc, :], T["qe"][:], ps["a"][:, 2 + fc % 2, :], OP.mult))
        Em.wait("tensor", "s_ws", 48)
        for tt in range(4):
            for half in range(2):
                bk = ("b", 0)
                tc = mm_acc(bk, ps["b"][:, 0, :],
                            [(T["h"][:, fc, tt * 128:(tt + 1) * 128],
                              T["swd"][:, fc, half * 512:(half + 1) * 512])
                             for fc in range(FC)])
                hdst = T["qe"] if half == 0 else T["qe2"]
                sq = "s_qa" if half == 0 else "s_qb"
                Em.wait("scalar", sq)
                evac("scalar", bk, tc, lambda e, hdst=hdst: e.activation(
                    hdst[:], ps["b"][:, 0, :], ACT.Copy))
                Em.dma("scalar", sq, smv[g * 4 + tt][:, half * 512:(half + 1) * 512], hdst[:])
        Em.wait("gpsimd", "t_mm", Em.cnt["t_mm"])

    Em.barrier()
    if STOP_AFTER == "ph3":
        _dbg_stop(Em, ap, dr, T, dbg)
        return

    Em.dma("gpsimd", "s_wq", T["wqkv"][:], ap["wqkv"].rearrange("(c p) n -> p c n", p=128))
    Em.wait("gpsimd", "s_wq")
    Em.wait("gpsimd", "s_cc", cc_xn)
    Em.wait("gpsimd", "s_zr")
    for g in range(8):
        Em.op("gpsimd", "s_gf", lambda e, g=g: e.dma_gather(
            out_ap=T["xnf_o"][:], in_ap=dr["ag_out_xn"].ap(),
            idxs_ap=T["batch_idx"][:, 128 + g * 16:128 + (g + 1) * 16], num_idxs=256,
            num_idxs_reg=256, elem_size=D, transpose=True), 16)
        gf = Em.cnt["s_gf"]
        xg = T["xnf_o"]
        Em.wait("tensor", "s_gf", gf)
        Em.wait("tensor", "s_wq", 32)
        for kc in range(DC):
            bk = ("a", kc % 4)
            tc = mm_acc(bk, ps["a"][:, kc % 4, 0:256],
                        [(T["wqkv"][:, dc, D + kc * 128:D + (kc + 1) * 128], xg[:, dc, :])
                         for dc in range(DC)])
            evac("vector", bk, tc, lambda e, g=g, kc=kc: e.tensor_copy(
                T["K"][:, 1, kc, g * 256:(g + 1) * 256], ps["a"][:, kc % 4, 0:256]))
        for tt in range(2):
            for vc in range(2):
                bk = ("b", 0)
                tc = mm_acc(bk, ps["b"][:, 0, :],
                            [(xg[:, dc, tt * 128:(tt + 1) * 128],
                              T["wqkv"][:, dc, 2 * D + vc * 512:2 * D + (vc + 1) * 512])
                             for dc in range(DC)])
                evac("vector", bk, tc, lambda e, g=g, tt=tt, vc=vc: e.tensor_copy(
                    T["V"][:, 1, g * 2 + tt, vc * 512:(vc + 1) * 512], ps["b"][:, 0, :]))
        Em.wait("gpsimd", "t_mm", Em.cnt["t_mm"])

    Em.barrier()
    if STOP_AFTER == "ph4":
        _dbg_stop(Em, ap, dr, T, dbg)
        return

    _routing(Em, ap, dr, T, ps, mm_acc, evac, transpose, bank_free, cc_lg)
    if STOP_AFTER in ("r0z", "r0a", "r1", "r2", "r3", "r4", "r5", "r6", "r7"):
        return

    Em.barrier()

    Em.dma("gpsimd", "s_wo", T["outw_n"][:], ap["outw"].rearrange("(c p) n -> p c n", p=128))
    Em.wait("tensor", "s_wo")
    if STOP_AFTER == "ph5w":
        return
    for r in range(DC):
        for c in range(DC):
            tslot = (r * DC + c) % 2
            tc = transpose(("t", tslot), ps["t"][:, tslot, 0:128],
                           T["outw_n"][:, r, c * 128:(c + 1) * 128], T["ident"][:])
            evac("vector", ("t", tslot), tc, lambda e, r=r, c=c, tslot=tslot: e.tensor_copy(
                T["woT"][:, c, r * 128:(r + 1) * 128], ps["t"][:, tslot, 0:128]))
            if STOP_AFTER == "w1" and r == 0 and c == 0:
                return
            if STOP_AFTER == "w4" and r == 0 and c == 3:
                return
        if STOP_AFTER == "w64" and r == DC - 1:
            return

    Em.barrier()
    if STOP_AFTER == "ph5":
        _dbg_stop(Em, ap, dr, T, dbg)
        return

    _attention(Em, ap, dr, T, ps, mm_acc, evac, transpose, bank_free)

    Em.barrier()
    if STOP_AFTER == "ph6":
        _dbg_stop(Em, ap, dr, T, dbg)
        return

    _experts(Em, ap, dr, T, ps, mm_acc, evac, bank_free)

    Em.barrier()
    if STOP_AFTER == "ph7":
        _dbg_stop(Em, ap, dr, T, dbg)
        return

    _combine(Em, ap, dr, T, ps)

    if dbg:
        _debug_out(Em, ap, dr, T, dbg)
        Em.wait("sync", "s_dbg")

    Em.wait("sync", "s_out")


def _routing(Em, ap, dr, T, ps, mm_acc, evac, transpose, bank_free, cc_lg):
    if STOP_AFTER == "r0z":
        return
    Em.wait("sync", "s_cc", cc_lg)
    if STOP_AFTER == "r0a":
        return
    srcv = dr["ag_out_lg"].ap().rearrange("(cb j) ce -> j cb ce", j=64)
    lg3 = T["lg_pm"].ap()
    lgv = lg3.rearrange("p (cb c) e -> p cb (c e)", cb=8)
    Em.dma("sync", "s_lg", lgv[0:64, :, :], srcv)
    Em.dma("sync", "s_lg", lgv[64:128, :, :], srcv)
    Em.wait("vector", "s_lg")
    if STOP_AFTER == "r0b":
        return

    def b16(apx, n=NCH):
        return apx.unsqueeze(2).broadcast_to([128, n, E])

    Em.op("vector", "v_ev", lambda e: e.reduce_max(T["m1pm"][:], lg3, axis=AX.X))
    Em.op("vector", "v_ev", lambda e: e.tensor_tensor(
        T["oh1"][:], lg3, b16(T["m1pm"][:]), OP.is_equal))
    Em.op("vector", "v_ev", lambda e: e.scalar_tensor_tensor(
        lg3, T["oh1"][:], -1e30, lg3, OP.mult, OP.add))
    Em.op("vector", "v_ev", lambda e: e.reduce_max(T["m2pm"][:], lg3, axis=AX.X))
    Em.op("vector", "v_ev", lambda e: e.tensor_tensor(
        T["oh2"][:], lg3, b16(T["m2pm"][:]), OP.is_equal))
    Em.op("vector", "v_ev", lambda e: e.tensor_scalar(
        T["ohk"][:], T["oh1"][:], T["even_col"][:], None, OP.mult))
    Em.op("vector", "v_ev", lambda e: e.scalar_tensor_tensor(
        T["ohk"][:], T["oh2"][:], T["odd_col"][:], T["ohk"][:], OP.mult, OP.add))
    vv = Em.cnt["v_ev"]
    if STOP_AFTER == "r1":
        return

    ohf = T["ohk"].ap().rearrange("p c e -> p (c e)")
    exf = T["excl"].ap().rearrange("p c e -> p (c e)")
    for j in range(8):
        bk = ("a", j % 4)
        tc = mm_acc(bk, ps["a"][:, j % 4, :],
                    [(T["lts"][:], ohf[:, j * 512:(j + 1) * 512])], waits=[("v_ev", vv)])
        evac("vector", bk, tc, lambda e, j=j: e.tensor_copy(
            exf[:, j * 512:(j + 1) * 512], ps["a"][:, j % 4, :]))
        bk2 = ("b", 0)
        tc2 = mm_acc(bk2, ps["b"][0:1, 0, :],
                     [(T["ones_col"][:], ohf[:, j * 512:(j + 1) * 512])],
                     waits=[("v_ev", vv)])
        evac("vector", bk2, tc2, lambda e, j=j: e.tensor_copy(
            T["trow"][0:1, j * 512:(j + 1) * 512], ps["b"][0:1, 0, :]))

    trow3 = T["trow"].ap().rearrange("a (c e) -> a c e", e=E)
    brow3 = T["brow"].ap().rearrange("a (c e) -> a c e", e=E)
    for ee in range(E):
        Em.op("vector", "v_ev", lambda e, ee=ee: e.tensor_tensor_scan(
            brow3[:, :, ee], trow3[:, :, ee], trow3[:, :, ee], 0.0, OP.add, OP.bypass))
    Em.op("vector", "v_ev", lambda e: e.tensor_tensor(
        T["brow"][:], T["brow"][:], T["trow"][:], OP.subtract))
    vb = Em.cnt["v_ev"]
    if STOP_AFTER == "r2":
        return

    for eloc in range(2):
        ec = T["ec0"] if eloc == 0 else T["ec1"]
        berow = T["be_row0"] if eloc == 0 else T["be_row1"]
        for hh in range(2):
            Em.op("vector", "v_ev", lambda e, hh=hh, ec=ec: e.tensor_tensor(
                T["scrH"][0:1, :, :], brow3[:, hh * 128:(hh + 1) * 128, :],
                ec.ap()[0:1, :].unsqueeze(1).broadcast_to([1, 128, E]), OP.mult))
            Em.op("vector", "v_ev", lambda e, hh=hh, berow=berow: e.reduce_sum(
                berow[0:1, hh * 128:(hh + 1) * 128], T["scrH"][0:1, :, :], axis=AX.X))

    Em.op("vector", "v_ev", lambda e: e.memset(T["pos_pair"][:], 0.0))
    for hh in range(2):
        sl = slice(hh * 128, (hh + 1) * 128)
        Em.op("vector", "v_ev", lambda e, sl=sl: e.tensor_tensor(
            T["scrH"][:], T["ohk"][:, sl, :], T["excl"][:, sl, :], OP.mult))
        Em.op("vector", "v_ev", lambda e, sl=sl: e.reduce_sum(
            T["pos_pair"][:, sl], T["scrH"][:], axis=AX.X))
    for ee in range(E):
        bk = ("c",)
        tc = mm_acc(bk, ps["c"][:, 0:NCH],
                    [(T["ones_row"][:], brow3[:, :, ee])], waits=[("v_ev", vb)])
        evac("vector", bk, tc, lambda e: e.tensor_copy(T["bbc"][:], ps["c"][:, 0:NCH]))
        Em.op("vector", "v_ev", lambda e, ee=ee: e.tensor_tensor(
            T["bbc"][:], T["ohk"][:, :, ee], T["bbc"][:], OP.mult))
        Em.op("vector", "v_ev", lambda e: e.tensor_add(
            T["pos_pair"][:], T["pos_pair"][:], T["bbc"][:]))
    vpp = Em.cnt["v_ev"]

    for k in range(2):
        selk = T["sel_k0"] if k == 0 else T["sel_k1"]
        for cc in range(2):
            bk = ("c",)
            tc = mm_acc(bk, ps["c"][:, 0:64],
                        [(T["pos_pair"][:, cc * 128:(cc + 1) * 128], selk[:])],
                        waits=[("v_ev", vpp)])
            evac("vector", bk, tc, lambda e, cc=cc: e.tensor_copy(
                T["posTj"][:, cc, :], ps["c"][:, 0:64]))
        vtj = Em.cnt["v_ev"]
        for par in range(2):
            mc = T["m_even"] if par == 0 else T["m_odd"]
            bk = ("c",)
            tc = mm_acc(bk, ps["c"][0:64, 0:16],
                        [(T["posTj"][:, cc, :], mc[:, cc, :]) for cc in range(2)],
                        waits=[("v_ev", vtj)])
            evac("vector", bk, tc, lambda e: e.tensor_copy(
                T["stg"][0:64, 0:16], ps["c"][0:64, 0:16]))
            vstg = Em.cnt["v_ev"]
            tc2 = transpose(("c",), ps["c"][0:16, 0:64], T["stg"][0:64, 0:16],
                            T["identf"][0:64, 0:64], waits=[("v_ev", vstg)])
            evac("vector", ("c",), tc2, lambda e, par=par: e.tensor_copy(
                T["stg2"][0:16, par * 64:(par + 1) * 64], ps["c"][0:16, 0:64]))
        vst = Em.cnt["v_ev"]
        tc3 = transpose(("c",), ps["c"][:, 0:16], T["stg2"][0:16, 0:128],
                        T["identf"][0:16, 0:16], waits=[("v_ev", vst)])
        evac("vector", ("c",), tc3, lambda e, k=k: e.tensor_copy(
            T["pos0" if k == 0 else "pos1"][:], ps["c"][:, 0:16]))
    if STOP_AFTER == "r3":
        return

    for eloc in range(2):
        ec = T["ec0"] if eloc == 0 else T["ec1"]
        member = T["member0"] if eloc == 0 else T["member1"]
        pose = T["pose0"] if eloc == 0 else T["pose1"]
        for hh in range(2):
            sl = slice(hh * 128, (hh + 1) * 128)
            Em.op("vector", "v_ev", lambda e, sl=sl, ec=ec: e.tensor_tensor(
                T["scrH"][:], T["ohk"][:, sl, :],
                ec.ap().unsqueeze(1).broadcast_to([128, 128, E]), OP.mult))
            Em.op("vector", "v_ev", lambda e, sl=sl, member=member: e.reduce_sum(
                member[:, sl], T["scrH"][:], axis=AX.X))
            Em.op("vector", "v_ev", lambda e, sl=sl, ec=ec: e.tensor_tensor(
                T["scrH"][:], T["excl"][:, sl, :],
                ec.ap().unsqueeze(1).broadcast_to([128, 128, E]), OP.mult))
            Em.op("vector", "v_ev", lambda e, sl=sl, pose=pose: e.reduce_sum(
                pose[:, sl], T["scrH"][:], axis=AX.X))

    lgo = T["lg_own"].ap()

    def bo(apx):
        return apx.unsqueeze(2).broadcast_to([128, 16, E])

    iE = T["iota_row"][:, 0:16].unsqueeze(1).broadcast_to([128, 16, E])
    s5 = T["scrH"].ap().rearrange("p c e -> p (c e)")
    Em.op("vector", "v_ev", lambda e: e.reduce_max(T["m1t"][:], lgo, axis=AX.X))
    Em.op("vector", "v_ev", lambda e: e.tensor_tensor(
        s5[:, 0:256].rearrange("p (c e) -> p c e", e=E), lgo, bo(T["m1t"][:]), OP.is_equal))
    Em.op("vector", "v_ev", lambda e: e.tensor_tensor(
        s5[:, 256:512].rearrange("p (c e) -> p c e", e=E),
        s5[:, 0:256].rearrange("p (c e) -> p c e", e=E), iE, OP.mult))
    Em.op("vector", "v_ev", lambda e: e.reduce_sum(
        T["e1t"][:], s5[:, 256:512].rearrange("p (c e) -> p c e", e=E), axis=AX.X))
    Em.op("vector", "v_ev", lambda e: e.tensor_tensor(
        s5[:, 256:512].rearrange("p (c e) -> p c e", e=E),
        s5[:, 0:256].rearrange("p (c e) -> p c e", e=E),
        T["parrow"].ap().unsqueeze(1).broadcast_to([128, 16, E]), OP.mult))
    Em.op("vector", "v_ev", lambda e: e.reduce_sum(
        T["par1"][:], s5[:, 256:512].rearrange("p (c e) -> p c e", e=E), axis=AX.X))
    Em.op("vector", "v_ev", lambda e: e.tensor_tensor(
        s5[:, 256:512].rearrange("p (c e) -> p c e", e=E),
        s5[:, 0:256].rearrange("p (c e) -> p c e", e=E),
        T["halfrow"].ap().unsqueeze(1).broadcast_to([128, 16, E]), OP.mult))
    Em.op("vector", "v_ev", lambda e: e.reduce_sum(
        T["rb1"][:], s5[:, 256:512].rearrange("p (c e) -> p c e", e=E), axis=AX.X))
    Em.op("vector", "v_ev", lambda e: e.scalar_tensor_tensor(
        s5[:, 512:768].rearrange("p (c e) -> p c e", e=E),
        s5[:, 0:256].rearrange("p (c e) -> p c e", e=E), -1e30, lgo, OP.mult, OP.add))
    Em.op("vector", "v_ev", lambda e: e.reduce_max(
        T["m2t"][:], s5[:, 512:768].rearrange("p (c e) -> p c e", e=E), axis=AX.X))
    Em.op("vector", "v_ev", lambda e: e.tensor_tensor(
        s5[:, 768:1024].rearrange("p (c e) -> p c e", e=E),
        s5[:, 512:768].rearrange("p (c e) -> p c e", e=E), bo(T["m2t"][:]), OP.is_equal))
    Em.op("vector", "v_ev", lambda e: e.tensor_tensor(
        s5[:, 1024:1280].rearrange("p (c e) -> p c e", e=E),
        s5[:, 768:1024].rearrange("p (c e) -> p c e", e=E), iE, OP.mult))
    Em.op("vector", "v_ev", lambda e: e.reduce_sum(
        T["e2t"][:], s5[:, 1024:1280].rearrange("p (c e) -> p c e", e=E), axis=AX.X))
    Em.op("vector", "v_ev", lambda e: e.tensor_tensor(
        s5[:, 1024:1280].rearrange("p (c e) -> p c e", e=E),
        s5[:, 768:1024].rearrange("p (c e) -> p c e", e=E),
        T["parrow"].ap().unsqueeze(1).broadcast_to([128, 16, E]), OP.mult))
    Em.op("vector", "v_ev", lambda e: e.reduce_sum(
        T["par2"][:], s5[:, 1024:1280].rearrange("p (c e) -> p c e", e=E), axis=AX.X))
    Em.op("vector", "v_ev", lambda e: e.tensor_tensor(
        s5[:, 1024:1280].rearrange("p (c e) -> p c e", e=E),
        s5[:, 768:1024].rearrange("p (c e) -> p c e", e=E),
        T["halfrow"].ap().unsqueeze(1).broadcast_to([128, 16, E]), OP.mult))
    Em.op("vector", "v_ev", lambda e: e.reduce_sum(
        T["rb2"][:], s5[:, 1024:1280].rearrange("p (c e) -> p c e", e=E), axis=AX.X))
    Em.op("vector", "v_ev", lambda e: e.tensor_tensor(
        T["scr16"][:], T["m2t"][:], T["m1t"][:], OP.subtract))
    Em.op("vector", "v_ev", lambda e: e.tensor_tensor(
        s5[:, 1024:1280].rearrange("p (c e) -> p c e", e=E), lgo, bo(T["m1t"][:]), OP.subtract))
    vexp = Em.cnt["v_ev"]
    Em.wait("scalar", "v_ev", vexp)
    Em.run("scalar", lambda e: e.preload_act_table(ACT.Exp))
    Em.op("scalar", "a_ev", lambda e: e.activation(T["scr16b"][:], T["scr16"][:], ACT.Exp))
    Em.op("scalar", "a_ev", lambda e: e.activation(
        s5[:, 1280:1536], s5[:, 1024:1280], ACT.Exp))
    aexp = Em.cnt["a_ev"]
    Em.wait("vector", "a_ev", aexp)
    Em.op("vector", "v_ev", lambda e: e.reduce_sum(
        T["zt"][:], s5[:, 1280:1536].rearrange("p (c e) -> p c e", e=E), axis=AX.X))
    Em.op("vector", "v_ev", lambda e: e.tensor_scalar(
        T["dent"][:], T["zt"][:], 1e-9, 1.0, OP.mult, OP.add))
    Em.op("vector", "v_ev", lambda e: e.tensor_add(T["dent"][:], T["dent"][:], T["scr16b"][:]))
    Em.op("vector", "v_ev", lambda e: e.reciprocal(T["w1t"][:], T["dent"][:]))
    Em.op("vector", "v_ev", lambda e: e.tensor_tensor(
        T["w2t"][:], T["scr16b"][:], T["w1t"][:], OP.mult))

    for k in range(2):
        et = T["e1t"] if k == 0 else T["e2t"]
        post = T["pos0"] if k == 0 else T["pos1"]
        wt = T["w1t"] if k == 0 else T["w2t"]
        wA = T["wA0" if k == 0 else "wA1"]
        wB = T["wB0" if k == 0 else "wB1"]
        iA = T["iA0" if k == 0 else "iA1"]
        iB = T["iB0" if k == 0 else "iB1"]
        win = T["win0" if k == 0 else "win1"]
        par = T["par1"] if k == 0 else T["par2"]
        rbase = T["rb1"] if k == 0 else T["rb2"]
        Em.op("vector", "v_ev", lambda e, post=post, win=win: e.tensor_scalar(
            win[:], post[:], float(CAP), None, OP.is_lt))
        Em.op("vector", "v_ev", lambda e, post=post, iA=iA, rbase=rbase: e.tensor_tensor(
            iA[:], rbase[:], post[:], OP.add))
        Em.op("vector", "v_ev", lambda e, iA=iA: e.tensor_scalar(
            iA[:], iA[:], float(NCORE * CAP - 1), 0.0, OP.min, OP.max))
        Em.op("vector", "v_ev", lambda e, iB=iB, iA=iA: e.tensor_copy(iB[:], iA[:]))
        Em.op("vector", "v_ev", lambda e, wt=wt, win=win: e.tensor_tensor(
            T["dent"][:], wt[:], win[:], OP.mult))
        Em.op("vector", "v_ev", lambda e, wB=wB, par=par: e.tensor_tensor(
            wB[:], T["dent"][:], par[:], OP.mult))
        Em.op("vector", "v_ev", lambda e, wA=wA, wB=wB: e.tensor_tensor(
            wA[:], T["dent"][:], wB[:], OP.subtract))
    vgi = Em.cnt["v_ev"]
    if STOP_AFTER == "r4":
        return

    idxp8 = T["idxpre"].ap().rearrange("p (r c) -> p r c", c=8)
    for nm_src, nm_dst in [("iA0", "cA0"), ("iB0", "cB0"), ("iA1", "cA1"), ("iB1", "cB1")]:
        for cc in range(8):
            bk = ("c",)
            tc = mm_acc(bk, ps["c"][0:16, 0:16],
                        [(T["identf"][:, cc * 16:(cc + 1) * 16], T[nm_src][:])],
                        waits=[("v_ev", vgi)])
            evac("vector", bk, tc, lambda e, cc=cc: e.tensor_copy(
                idxp8[0:16, 0:16, cc], ps["c"][0:16, 0:16]))
        Em.op("vector", "v_ev", lambda e, nm_dst=nm_dst: e.tensor_copy(
            T[nm_dst][0:16, :], T["idxpre"][0:16, 0:128]))
        vrep = Em.cnt["v_ev"]
        Em.wait("sync", "v_ev", vrep)
        for r in range(1, 8):
            Em.dma("sync", "s_rep", T[nm_dst][16 * r:16 * (r + 1), :], T[nm_dst][0:16, :])

    if STOP_AFTER == "r5":
        return
    last_oh_tmm = {}
    for eloc in range(2):
        member = T["member0"] if eloc == 0 else T["member1"]
        pose = T["pose0"] if eloc == 0 else T["pose1"]
        berow = T["be_row0"] if eloc == 0 else T["be_row1"]
        bk = ("c",)
        tc = mm_acc(bk, ps["c"][:, 0:NCH], [(T["ones_row"][:], berow[0:1, :])],
                    waits=[("v_ev", Em.cnt["v_ev"])])
        evac("vector", bk, tc, lambda e: e.tensor_copy(T["bbc"][:], ps["c"][:, 0:NCH]))
        Em.op("vector", "v_ev", lambda e, pose=pose: e.tensor_add(
            pose[:], pose[:], T["bbc"][:]))
        Em.op("vector", "v_ev", lambda e, pose=pose: e.tensor_scalar(
            T["val"][:], pose[:], float(SLOTPAD), None, OP.is_lt))
        Em.op("vector", "v_ev", lambda e, member=member: e.tensor_tensor(
            T["val"][:], T["val"][:], member[:], OP.mult))
        Em.op("vector", "v_ev", lambda e: e.scalar_tensor_tensor(
            T["val"][:], T["tokbase"][:], 1.0, T["val"][:], OP.add, OP.mult))
        for qq in range(4):
            sl4 = slice(qq * 64, (qq + 1) * 64)
            Em.op("vector", "v_ev", lambda e, sl4=sl4, pose=pose: e.tensor_tensor(
                T["scrH"].ap().rearrange("p c e -> p (c e)")[:, 0:64 * ROWS]
                 .rearrange("p (c r) -> p c r", r=ROWS),
                pose[:, sl4].unsqueeze(2).broadcast_to([128, 64, ROWS]),
                T["thresh"].ap().unsqueeze(1).broadcast_to([128, 64, ROWS]), OP.is_ge))
            Em.op("vector", "v_ev", lambda e, sl4=sl4: e.reduce_sum(
                T["rowi"][:, sl4],
                T["scrH"].ap().rearrange("p c e -> p (c e)")[:, 0:64 * ROWS]
                 .rearrange("p (c r) -> p c r", r=ROWS), axis=AX.X))
        Em.op("vector", "v_ev", lambda e, pose=pose: e.scalar_tensor_tensor(
            T["posm"][:], T["rowi"][:], -128.0, pose[:], OP.mult, OP.add))
        abank = ("a", eloc)
        if abank in bank_free:
            Em.wait("tensor", bank_free[abank][0], bank_free[abank][1])
        sbank = ("b", 0)
        if sbank in bank_free:
            Em.wait("tensor", bank_free[sbank][0], bank_free[sbank][1])
        for cq in range(64):
            slot = cq % 4
            if slot in last_oh_tmm:
                Em.wait("vector", "t_mm", last_oh_tmm[slot])
            Em.op("vector", "v_ev", lambda e, cq=cq, slot=slot: e.tensor_tensor(
                T["ohsl"][:, slot, :, :],
                T["iota_row"][:, 0:128].unsqueeze(1).broadcast_to([128, 4, 128]),
                T["posm"][:, cq * 4:(cq + 1) * 4].unsqueeze(2).broadcast_to([128, 4, 128]),
                OP.is_equal))
            Em.op("vector", "v_ev", lambda e, cq=cq, slot=slot: e.tensor_tensor(
                T["scrH"].ap().rearrange("p c e -> p (c e)")[:, slot * 128:slot * 128 + 4 * ROWS]
                 .rearrange("p (c r) -> p c r", r=ROWS),
                T["rowi"][:, cq * 4:(cq + 1) * 4].unsqueeze(2).broadcast_to([128, 4, ROWS]),
                T["iota_row"][:, 0:ROWS].unsqueeze(1).broadcast_to([128, 4, ROWS]),
                OP.is_equal))
            Em.op("vector", "v_ev", lambda e, cq=cq, slot=slot: e.tensor_tensor(
                T["scrH"].ap().rearrange("p c e -> p (c e)")[:, slot * 128:slot * 128 + 4 * ROWS]
                 .rearrange("p (c r) -> p c r", r=ROWS),
                T["scrH"].ap().rearrange("p c e -> p (c e)")[:, slot * 128:slot * 128 + 4 * ROWS]
                 .rearrange("p (c r) -> p c r", r=ROWS),
                T["val"][:, cq * 4:(cq + 1) * 4].unsqueeze(2).broadcast_to([128, 4, ROWS]),
                OP.mult))
            voh = Em.cnt["v_ev"]
            Em.wait("tensor", "v_ev", voh)
            for j in range(4):
                ch = cq * 4 + j
                Em.op("tensor", "t_mm", lambda e, slot=slot, j=j, ch=ch: e.matmul(
                    ps["b"][:, 0, 0:ROWS],
                    T["ohsl"][:, slot, j, :],
                    T["scrH"].ap().rearrange("p c e -> p (c e)")[:, slot * 128 + j * ROWS:slot * 128 + (j + 1) * ROWS],
                    start=(ch == 0), stop=(ch == NCH - 1), skip_group_check=True))
            last_oh_tmm[slot] = Em.cnt["t_mm"]
        tred = Em.cnt["t_mm"]
        evac("vector", sbank, tred, lambda e: e.tensor_copy(
            T["sAT"][:, 0:ROWS], ps["b"][:, 0, 0:ROWS]))
        vsat = Em.cnt["v_ev"]
        idxp8b = T["idxpre"].ap().rearrange("p (r c) -> p r c", c=8)
        for cc in range(8):
            bk = ("c",)
            tc = mm_acc(bk, ps["c"][0:16, 0:ROWS],
                        [(T["identf"][:, cc * 16:(cc + 1) * 16], T["sAT"][:, 0:ROWS])],
                        waits=[("v_ev", vsat)])
            evac("vector", bk, tc, lambda e, cc=cc: e.tensor_copy(
                idxp8b[0:16, 0:ROWS, cc], ps["c"][0:16, 0:ROWS]))
        eidx_t = T["eidx0"] if eloc == 0 else T["eidx1"]
        Em.op("vector", "v_ev", lambda e, eidx_t=eidx_t: e.tensor_copy(
            eidx_t[0:16, :], T["idxpre"][0:16, 0:160]))
        vei = Em.cnt["v_ev"]
        Em.wait("sync", "v_ev", vei)
        for r in range(1, 8):
            Em.dma("sync", "s_rep", eidx_t[16 * r:16 * (r + 1), :], eidx_t[0:16, :])
        if STOP_AFTER == "r6" and eloc == 0:
            return



def _dbg_stop(Em, ap, dr, T, dbg):
    if dbg:
        _debug_out(Em, ap, dr, T, dbg)
        Em.wait("sync", "s_dbg")

def _attention(Em, ap, dr, T, ps, mm_acc, evac, transpose, bank_free):
    Em.run("scalar", lambda e: e.preload_act_table(ACT.Exp))
    Em.wait("sync", "s_qa")
    Em.wait("sync", "s_qb")
    qsv = dr["q_scr"].ap()
    asv = dr["attn_scr"].ap().rearrange("(i p) d -> i p d", p=128)
    qk_tmm = {}

    def issue_q(qt):
        if qt >= 16:
            return
        if qt >= 2:
            Em.wait("sync", "t_mm", qk_tmm[qt - 2])
        Em.dma("sync", "s_qt", (T["Qt"] if qt % 2 == 0 else T["Qt2"])[:],
               qsv[:, :, qt * 128:(qt + 1) * 128].rearrange("c p t -> p c t"))

    issue_q(0)
    issue_q(1)
    for g in range(8):
        for sub in range(2):
            qt = g * 2 + sub
            Em.wait("tensor", "s_qt", 16 * (qt + 1))
            Qb = T["Qt"] if qt % 2 == 0 else T["Qt2"]
            for kc in range(8):
                half, kk = divmod(kc, 4)
                bk = ("a", kc % 4)
                tc = mm_acc(bk, ps["a"][:, kc % 4, :],
                            [(Qb[:, dc, :], T["K"][:, half, dc, kk * 512:(kk + 1) * 512])
                             for dc in range(DC)])
                if kc == 7:
                    qk_tmm[qt] = Em.cnt["t_mm"]
                    issue_q(qt + 1)
                if kc % 2 == 0:
                    evac("vector", bk, tc, lambda e, kc=kc: e.tensor_copy(
                        T["S"][:, kc * 512:(kc + 1) * 512], ps["a"][:, kc % 4, :]))
                else:
                    evac("scalar", bk, tc, lambda e, kc=kc: e.activation(
                        T["S"][:, kc * 512:(kc + 1) * 512], ps["a"][:, kc % 4, :], ACT.Copy))
            Em.wait("vector", "a_ev", Em.cnt["a_ev"])
            Em.op("vector", "v_ev", lambda e, qt=qt: e.reduce_max(
                T["attn_m"][:, qt:qt + 1], T["S"][:], axis=AX.X, negate=True))
            vmax = Em.cnt["v_ev"]
            Em.wait("scalar", "v_ev", vmax)
            Em.op("scalar", "a_ev", lambda e, qt=qt: e.activation(
                T["P"][:], T["S"][:], ACT.Exp, bias=T["attn_m"][:, qt:qt + 1],
                accum_out=T["attn_l"][:, qt:qt + 1]))
            aexp = Em.cnt["a_ev"]
            for kt in range(32):
                tslot = kt % 2
                tc = transpose(("t", tslot), ps["t"][:, tslot, 0:128],
                               T["P"][:, kt * 128:(kt + 1) * 128], T["ident"][:],
                               waits=[("a_ev", aexp)])
                if kt % 2 == 0:
                    evac("vector", ("t", tslot), tc, lambda e, kt=kt, sub=sub, tslot=tslot:
                         e.tensor_copy(T["PT"][:, kt, sub * 128:(sub + 1) * 128],
                                       ps["t"][:, tslot, 0:128]))
                else:
                    evac("scalar", ("t", tslot), tc, lambda e, kt=kt, sub=sub, tslot=tslot:
                         e.activation(T["PT"][:, kt, sub * 128:(sub + 1) * 128],
                                      ps["t"][:, tslot, 0:128], ACT.Copy))
        vpt = Em.cnt["v_ev"]
        apt = Em.cnt["a_ev"]
        for dc in range(DC):
            bk = ("a", 2 + dc % 2)
            pairs = []
            for kt in range(32):
                half, tt = divmod(kt, 16)
                pairs.append((T["V"][:, half, tt, dc * 128:(dc + 1) * 128], T["PT"][:, kt, :]))
            tc = mm_acc(bk, ps["a"][:, 2 + dc % 2, 0:256], pairs,
                        waits=[("v_ev", vpt), ("a_ev", apt)])
            evac("vector", bk, tc, lambda e, dc=dc: e.tensor_copy(
                T["Osb"][:, dc, :], ps["a"][:, 2 + dc % 2, 0:256]))
        Em.op("vector", "v_ev", lambda e, g=g: e.reciprocal(
            T["linv"][:, g * 2:g * 2 + 2], T["attn_l"][:, g * 2:g * 2 + 2]))
        vo = Em.cnt["v_ev"]
        for sub in range(2):
            qt = g * 2 + sub
            for oc in range(2):
                bk = ("a", oc)
                tc = mm_acc(bk, ps["a"][:, oc, :],
                            [(T["Osb"][:, dc, sub * 128:(sub + 1) * 128],
                              T["woT"][:, dc, oc * 512:(oc + 1) * 512])
                             for dc in range(DC)], waits=[("v_ev", vo)])
                qdst = T["qe"] if oc == 0 else T["qe2"]
                sq = "s_qa" if oc == 0 else "s_qb"
                Em.wait("scalar", sq)
                evac("scalar", bk, tc, lambda e, qt=qt, qdst=qdst, oc=oc: e.activation(
                    qdst[:], ps["a"][:, oc, :], ACT.Copy,
                    scale=T["linv"][:, qt:qt + 1]))
                Em.dma("scalar", sq, asv[qt][:, oc * 512:(oc + 1) * 512], qdst[:])


def _experts(Em, ap, dr, T, ps, mm_acc, evac, bank_free):
    Em.run("scalar", lambda e: e.preload_act_table(ACT.Silu))
    Em.op("vector", "v_ev", lambda e: e.memset(T["einp"][:], 0.0))
    vz = Em.cnt["v_ev"]
    for eloc in range(2):
        Em.dma("gpsimd", "s_we", T["ewg"][:],
               ap["wg"][eloc].rearrange("(c p) n -> p c n", p=128))
        Em.wait("gpsimd", "s_we")
        Em.dma("gpsimd", "s_we", T["ewu"][:],
               ap["wu"][eloc].rearrange("(c p) n -> p c n", p=128))
        Em.wait("gpsimd", "s_we")
        Em.dma("gpsimd", "s_we", T["ewd"][:],
               ap["wd"][eloc].rearrange("(c p) n -> p c n", p=128))
        Em.wait("gpsimd", "s_we")
        we_cnt = Em.cnt["s_we"]
        eidx_t = T["eidx0"] if eloc == 0 else T["eidx1"]
        Em.wait("gpsimd", "v_ev", vz)
        Em.wait("gpsimd", "t_mm", Em.cnt["t_mm"])
        Em.wait("gpsimd", "s_zr")
        Em.wait("gpsimd", "s_rep")
        Em.wait("gpsimd", "s_cc", 1)
        for qf in range(4):
            Em.op("gpsimd", "s_gf", lambda e, eidx_t=eidx_t, qf=qf: e.dma_gather(
                out_ap=T["einp"][:, qf, :, :],
                in_ap=dr["ag_out_xn"].ap(),
                idxs_ap=eidx_t[:, qf * 40:(qf + 1) * 40], num_idxs=640,
                num_idxs_reg=640, elem_size=D, transpose=True), 16)
            Em.wait("gpsimd", "s_gf")
        gf = Em.cnt["s_gf"]
        eov = (dr["ag_in_eoA"] if eloc == 0 else dr["ag_in_eoB"]).ap() \
            .rearrange("(i p) d -> i p d", p=128)
        Em.wait("tensor", "s_gf", gf)
        Em.wait("tensor", "s_we", we_cnt)
        chunks = [(q, scl, csz) for q in range(4) for (scl, csz) in [(0, 512), (512, 128)]]
        for sc, (hf, scl, csz) in enumerate(chunks):
            xg = T["einp"][:, hf, :, scl:scl + csz]
            for fc in range(FC):
                bkg = ("a", fc % 2)
                tcg = mm_acc(bkg, ps["a"][:, fc % 2, 0:csz],
                             [(T["ewg"][:, dc, fc * 128:(fc + 1) * 128], xg[:, dc, :])
                              for dc in range(DC)])
                bku = ("a", 2 + fc % 2)
                tcu = mm_acc(bku, ps["a"][:, 2 + fc % 2, 0:csz],
                             [(T["ewu"][:, dc, fc * 128:(fc + 1) * 128], xg[:, dc, :])
                              for dc in range(DC)])
                Em.wait("scalar", "s_qa")
                evac("scalar", bkg, tcg, lambda e, csz=csz, fc=fc: e.activation(
                    T["qe"][:, 0:csz], ps["a"][:, fc % 2, 0:csz], ACT.Silu))
                Em.wait("vector", "a_ev", Em.cnt["a_ev"])
                evac("vector", bku, tcu, lambda e, fc=fc, csz=csz: e.tensor_tensor(
                    T["eh"][:, fc, 0:csz], T["qe"][:, 0:csz],
                    ps["a"][:, 2 + fc % 2, 0:csz], OP.mult))
            for tt in range(csz // 128):
                Em.wait("vector", "s_qa" if tt % 2 == 0 else "s_qb")
                for half in range(2):
                    bk = ("b", 0)
                    tc = mm_acc(bk, ps["b"][:, 0, :],
                                [(T["eh"][:, fc, tt * 128:(tt + 1) * 128],
                                  T["ewd"][:, fc, half * 512:(half + 1) * 512])
                                 for fc in range(FC)])
                    evac("vector", bk, tc, lambda e, tt=tt, half=half: e.tensor_copy(
                        T["eo_t"][:, tt % 2, half * 512:(half + 1) * 512],
                        ps["b"][:, 0, :]))
                Em.wait("scalar", "v_ev", Em.cnt["v_ev"])
                rowt = (hf * 640 + scl) // 128 + tt
                Em.dma("scalar", "s_qa" if tt % 2 == 0 else "s_qb",
                       eov[rowt], T["eo_t"][:, tt % 2, :])
        Em.wait("gpsimd", "s_qa")
        Em.wait("gpsimd", "s_qb")
        Em.op("gpsimd", "s_cc", lambda e, eloc=eloc: e.collective_compute(
            "AllGather", OP.bypass, replica_groups=[list(range(NCORE))],
            ins=[(dr["ag_in_eoA"] if eloc == 0 else dr["ag_in_eoB"]).ap().opt()],
            outs=[(dr["ag_out_eoA"] if eloc == 0 else dr["ag_out_eoB"]).ap().opt()]))


def _combine(Em, ap, dr, T, ps):
    Em.wait("gpsimd", "s_cc")
    for nm_idx, nm_buf, src in [("cA0", "gA0", "ag_out_eoA"), ("cB0", "gB0", "ag_out_eoB"),
                                ("cA1", "gA1", "ag_out_eoA"), ("cB1", "gB1", "ag_out_eoB")]:
        for hf in range(2):
            Em.op("gpsimd", "s_gf", lambda e, nm_idx=nm_idx, nm_buf=nm_buf, src=src, hf=hf:
                  e.dma_gather(
                      out_ap=T[nm_buf][:, hf * 8:(hf + 1) * 8, :], in_ap=dr[src].ap(),
                      idxs_ap=T[nm_idx][:, hf * 64:(hf + 1) * 64], num_idxs=NT // 2,
                      num_idxs_reg=NT // 2, elem_size=D, transpose=False), 16)
            Em.wait("gpsimd", "s_gf")
    gf = Em.cnt["s_gf"]
    Em.wait("sync", "s_qa")
    Em.wait("sync", "s_qb")
    Em.dma("sync", "s_ob", T["outb"][:], ap["outb"])
    xv = ap["x"].rearrange("(i p) d -> i p d", p=128)
    asv = dr["attn_scr"].ap().rearrange("(i p) d -> i p d", p=128)
    smv = dr["smlp_scr"].ap().rearrange("(i p) d -> i p d", p=128)
    ov = ap["out"].rearrange("(i p) d -> i p d", p=128)
    Em.wait("vector", "s_gf", gf)
    Em.wait("vector", "s_ob")
    for i in range(16):
        sx = "s_xa" if i % 2 == 0 else "s_xb"
        Em.dma("sync", sx, T["cx"][:, i % 2, :], xv[i])
        Em.dma("sync", sx, T["ca"][:, i % 2, :], asv[i])
        Em.dma("sync", sx, T["cs"][:, i % 2, :], smv[i])
        Em.wait("vector", sx)

        def comb(e, i=i):
            e.tensor_scalar(T["acc"][:], T["cx"][:, i % 2, :],
                            T["isg1"][:, i:i + 1], T["nms"][:, i:i + 1], OP.mult, OP.add)
            e.tensor_add(T["acc"][:], T["acc"][:], T["outb"][:])
            e.tensor_add(T["acc"][:], T["acc"][:], T["ca"][:, i % 2, :])
            e.tensor_add(T["acc"][:], T["acc"][:], T["cs"][:, i % 2, :])
            e.scalar_tensor_tensor(T["acc"][:], T["gA0"][:, i, :], T["wA0"][:, i:i + 1],
                                   T["acc"][:], OP.mult, OP.add)
            e.scalar_tensor_tensor(T["acc"][:], T["gB0"][:, i, :], T["wB0"][:, i:i + 1],
                                   T["acc"][:], OP.mult, OP.add)
            e.scalar_tensor_tensor(T["acc"][:], T["gA1"][:, i, :], T["wA1"][:, i:i + 1],
                                   T["acc"][:], OP.mult, OP.add)
            return e.scalar_tensor_tensor(T["acc"][:], T["gB1"][:, i, :], T["wB1"][:, i:i + 1],
                                          T["acc"][:], OP.mult, OP.add)
        Em.op("vector", "v_ev", comb)
        Em.wait("sync", "v_ev")
        Em.dma("sync", "s_out", ov[i], T["acc"][:])


def _debug_out(Em, ap, dr, T, dbg):
    def d2d(dst, src):
        Em.dma("sync", "s_dbg", dst, src)
    d2d(dbg["lg"], dr["ag_out_lg"].ap())
    d2d(dbg["q"], dr["q_scr"].ap())
    d2d(dbg["attn"], dr["attn_scr"].ap())
    d2d(dbg["smlp"], dr["smlp_scr"].ap())
    d2d(dbg["eoA"], dr["ag_in_eoA"].ap())
    d2d(dbg["eoB"], dr["ag_in_eoB"].ap())
    d2d(dbg["K"], T["K"].ap().rearrange("p a c t -> p (a c t)"))
    d2d(dbg["V"], T["V"].ap().rearrange("p a c t -> p (a c t)"))
    d2d(dbg["einp"], T["einp"][:, 0, :, :].rearrange("p c t -> p (c t)"))
    for j, nm in enumerate(["mu", "isg", "st2", "st3", "nms", "isg1"]):
        d2d(dbg["stats"][:, j, :], T[nm][:])
    d2d(dbg["pos"][:, 0, :], T["pos0"][:])
    d2d(dbg["pos"][:, 1, :], T["pos1"][:])
    for j, nm in enumerate(["wA0", "wB0", "wA1", "wB1", "iA0", "iB0", "iA1", "iB1",
                            "e1t", "e2t", "w1t", "w2t"]):
        d2d(dbg["wm"][:, j, :], T[nm][:])
    d2d(dbg["eidx"][:, 0, :], T["eidx0"][:])
    d2d(dbg["eidx"][:, 1, :], T["eidx1"][:])



_prog_cache = {}


def kernel(L0_attn_x, L0_attn_weight_qkv, L0_attn_out_w, L0_attn_out_b,
           L0_ffn_router_w, L0_ffn_router_b, L0_ffn_w_gate, L0_ffn_w_up,
           L0_ffn_w_down, L0_ffn_sw_gate, L0_ffn_sw_up, L0_ffn_sw_down):
    x = np.ascontiguousarray(np.asarray(L0_attn_x, dtype=np.float32)).reshape(N, D)
    wqkv = np.ascontiguousarray(np.asarray(L0_attn_weight_qkv, np.float32))
    outw = np.ascontiguousarray(np.asarray(L0_attn_out_w, np.float32))
    outb = np.asarray(L0_attn_out_b, np.float32)
    rw = np.ascontiguousarray(np.asarray(L0_ffn_router_w, np.float32))
    rb = np.asarray(L0_ffn_router_b, np.float32)
    wg = np.ascontiguousarray(np.asarray(L0_ffn_w_gate, np.float32))
    wu = np.ascontiguousarray(np.asarray(L0_ffn_w_up, np.float32))
    wd = np.ascontiguousarray(np.asarray(L0_ffn_w_down, np.float32))
    swg = np.ascontiguousarray(np.asarray(L0_ffn_sw_gate, np.float32))
    swu = np.ascontiguousarray(np.asarray(L0_ffn_sw_up, np.float32))
    swd = np.ascontiguousarray(np.asarray(L0_ffn_sw_down, np.float32))

    if "nc" not in _prog_cache:
        _prog_cache["nc"] = build_program()
    nc = _prog_cache["nc"]

    outb_bc = np.ascontiguousarray(np.tile(outb[None, :], (128, 1)))
    rb_bc = np.ascontiguousarray(np.tile(rb[None, :], (128, 1)))

    in_maps = []
    for c in range(NCORE):
        consts = build_consts(c)
        m = {
            "x_own": np.ascontiguousarray(x[c * NT:(c + 1) * NT]),
            "w_qkv": wqkv, "out_w": outw, "out_b_bc": outb_bc,
            "router_w": rw, "router_b_bc": rb_bc,
            "wg_pair": np.ascontiguousarray(wg[2 * c:2 * c + 2]),
            "wu_pair": np.ascontiguousarray(wu[2 * c:2 * c + 2]),
            "wd_pair": np.ascontiguousarray(wd[2 * c:2 * c + 2]),
            "sw_gate": swg, "sw_up": swu, "sw_down": swd,
        }
        for k in CONST_SHAPES:
            m["c_" + k] = np.ascontiguousarray(consts[k])
        m["c_m_even"] = np.ascontiguousarray(consts["m_even"])
        m["c_m_odd"] = np.ascontiguousarray(consts["m_odd"])
        in_maps.append(m)

    res = run_bass_kernel_spmd(nc, in_maps, core_ids=list(range(NCORE)))
    out = np.concatenate([res.results[c]["out"] for c in range(NCORE)], axis=0)
    if DBG:
        kernel._dbg = res.results
    return out.reshape(4, 4096, D)
